# version 01
# baseline (speedup 1.0000x reference)
import os
import numpy as np
import ml_dtypes
import concourse.bass as bass
import concourse.mybir as mybir
from concourse.bass_utils import run_bass_kernel_spmd
from contextlib import ExitStack

F32 = mybir.dt.float32
BF16 = mybir.dt.bfloat16
U32 = mybir.dt.uint32
AF = mybir.ActivationFunctionType
ALU = mybir.AluOpType
AX = mybir.AxisListType

T = 2048
NT = 16
D = 1024
ND = 8
H = 16
DH = 64
FF = 4096
NE = 8
KCAP = 512
CTX = 1024
NPOOL = 256
EPS = 1e-5
SHIFT = 12.0
BISECT_ITERS = 40

ENGS = ("tensor", "vector", "scalar", "gpsimd", "sync")

W_NAMES = [
    "aaWq", "aaWk", "aaWv", "aaWo", "aaW1", "aab1", "aaW2", "aab2",
    "abWq", "abbq", "abWl", "abbl", "abWk", "abbk", "abWv", "abbv",
    "abWo", "abbo", "gw", "w1", "w2",
]


class Sem:

    def __init__(self, es, nc, name):
        self.h = es.enter_context(nc.semaphore(name))
        self.n = 0
        self.name = name


class Prog:

    def __init__(self):
        self.q = {e: [] for e in ENGS}

    def op(self, eng, f, waits=(), inc=None, inc_n=1):
        wl = tuple((s.h, int(v)) for (s, v) in waits)
        if inc is not None:
            inc.n += inc_n
            ih = inc.h
        else:
            ih = None

        def run(e, f=f, wl=wl, ih=ih, inc_n=inc_n):
            for h, v in wl:
                if v > 0:
                    e.wait_ge(h, v)
            inst = f(e)
            if ih is not None:
                assert inst is not None
                inst.then_inc(ih, inc_n)

        self.q[eng].append(run)

    def wait(self, eng, waits):
        self.op(eng, lambda e: None, waits=waits)


class _StopEmission(Exception):
    pass


KNOBS = set(os.environ.get("KNOBS", "").split(","))


def build_nc(n_sub=4, stop_after=None):
    nc = bass.Bass("TRN2", target_bir_lowering=False,
                   detect_race_conditions=False)

    xin = nc.dram_tensor("xin", [2, T, D], F32, kind="ExternalInput")
    y = nc.dram_tensor("y", [2, T, D], F32, kind="ExternalOutput")
    dbg = nc.dram_tensor("dbg", [128, 16384], F32, kind="ExternalOutput")
    dw = {}
    dw["aaWq"] = nc.dram_tensor("aaWq", [D, D], F32, kind="ExternalInput")
    dw["aaWk"] = nc.dram_tensor("aaWk", [D, D], F32, kind="ExternalInput")
    dw["aaWv"] = nc.dram_tensor("aaWv", [D, D], F32, kind="ExternalInput")
    dw["aaWo"] = nc.dram_tensor("aaWo", [D, D], F32, kind="ExternalInput")
    dw["aaW1"] = nc.dram_tensor("aaW1", [D, 512], F32, kind="ExternalInput")
    dw["aab1"] = nc.dram_tensor("aab1", [512], F32, kind="ExternalInput")
    dw["aaW2"] = nc.dram_tensor("aaW2", [512, D], F32, kind="ExternalInput")
    dw["aab2"] = nc.dram_tensor("aab2", [D], F32, kind="ExternalInput")
    dw["abWq"] = nc.dram_tensor("abWq", [D, D], F32, kind="ExternalInput")
    dw["abbq"] = nc.dram_tensor("abbq", [D], F32, kind="ExternalInput")
    dw["abWl"] = nc.dram_tensor("abWl", [D, 256], F32, kind="ExternalInput")
    dw["abbl"] = nc.dram_tensor("abbl", [256], F32, kind="ExternalInput")
    dw["abWk"] = nc.dram_tensor("abWk", [256, D], F32, kind="ExternalInput")
    dw["abbk"] = nc.dram_tensor("abbk", [D], F32, kind="ExternalInput")
    dw["abWv"] = nc.dram_tensor("abWv", [256, D], F32, kind="ExternalInput")
    dw["abbv"] = nc.dram_tensor("abbv", [D], F32, kind="ExternalInput")
    dw["abWo"] = nc.dram_tensor("abWo", [D, D], F32, kind="ExternalInput")
    dw["abbo"] = nc.dram_tensor("abbo", [D], F32, kind="ExternalInput")
    dw["gw"] = nc.dram_tensor("gw", [NE, D], F32, kind="ExternalInput")
    dw["w1"] = nc.dram_tensor("w1", [NE, D, FF], F32, kind="ExternalInput")
    dw["w2"] = nc.dram_tensor("w2", [NE, FF, D], F32, kind="ExternalInput")
    c_ident = nc.dram_tensor("c_ident", [128, 128], BF16, kind="ExternalInput")
    c_identf = nc.dram_tensor("c_identf", [128, 128], F32, kind="ExternalInput")
    c_m0 = nc.dram_tensor("c_m0", [128, 256], BF16, kind="ExternalInput")
    c_m1 = nc.dram_tensor("c_m1", [128, 256], BF16, kind="ExternalInput")
    c_idxp = nc.dram_tensor("c_idxp", [128, 32], BF16, kind="ExternalInput")
    d_ln = nc.dram_tensor("d_ln", [T, D], BF16, kind="Internal")
    d_probs = nc.dram_tensor("d_probs", [T, NE], F32, kind="Internal")
    d_acc = nc.dram_tensor("d_acc", [T, D], F32, kind="Internal")

    with ExitStack() as es:
        def sbt(name, shape, dt):
            return es.enter_context(nc.sbuf_tensor(name, shape, dt))

        def pst(name, shape, dt):
            return es.enter_context(nc.psum_tensor(name, shape, dt))

        x_res = sbt("x_res", [128, NT * D], F32)
        A = sbt("A", [128, 16384], BF16)
        B = sbt("B", [128, 16384], BF16)
        C = sbt("C", [128, 16640], BF16)
        Dn_t = sbt("Dn", [128, 4096], F32)
        Dn = Dn_t[:].bitcast(BF16)
        W = sbt("W", [128, 8192], BF16)
        E = sbt("E", [128, 5120], BF16)
        ident = sbt("ident", [128, 128], BF16)
        m0 = sbt("m0", [128, 256], BF16)
        m1 = sbt("m1", [128, 256], BF16)
        idxp = sbt("idxp", [128, 32], BF16)
        gateT = sbt("gateT", [128, 64], BF16)
        ones1 = sbt("ones1", [1, 128], F32)
        bias_a = sbt("bias_a", [128, 12], F32)
        bias_b = sbt("bias_b", [128, 18], F32)
        lncol = sbt("lncol", [128, 160], F32)
        probs = sbt("probs", [128, 128], F32)
        rankt = sbt("rankt", [128, 128], F32)
        gall = sbt("gall", [128, 256], F32)
        idxall = sbt("idxall", [128, 32], U32)
        idxf = sbt("idxf", [128, 8], F32)
        smallf = sbt("smallf", [128, 32], F32)
        bis = sbt("bis", [8, 8], F32)

        P = [pst(f"P{i}", [128, 512], F32) for i in range(8)]

        sT = Sem(es, nc, "sT")
        sV = Sem(es, nc, "sV")
        sS = Sem(es, nc, "sS")
        sG = Sem(es, nc, "sG")
        qSY = Sem(es, nc, "qSY")
        qGW = Sem(es, nc, "qGW")
        qGM = Sem(es, nc, "qGM")
        qGS = Sem(es, nc, "qGS")
        qW1 = [Sem(es, nc, f"qW1{i}") for i in range(2)]
        qW2 = [Sem(es, nc, f"qW2{i}") for i in range(2)]
        qXI = [Sem(es, nc, f"qXI{i}") for i in range(2)]

        P_ = Prog()

        def TOP(eng, f, waits=(), inc=None, inc_n=1):
            P_.op(eng, f, waits=waits, inc=inc, inc_n=inc_n)

        def barrier():
            if stop_after == "setup_nball" or "nob" in KNOBS:
                return
            for eng, s in (("vector", sV), ("scalar", sS), ("gpsimd", sG)):
                TOP(eng, lambda e: e.nop(), inc=s)
            tgt = [(sT, sT.n), (sV, sV.n), (sS, sS.n), (sG, sG.n)]
            if "nobd" not in KNOBS:
                tgt += [(qSY, qSY.n), (qGW, qGW.n), (qGM, qGM.n), (qGS, qGS.n)]
                tgt += [(q, q.n) for q in qW1 + qW2 + qXI]
            for eng in ENGS:
                P_.wait(eng, tgt)

        def xt(s_unused, t):
            return x_res[:, D * t:D * (t + 1)]

        def check_stop(name):
            if stop_after is not None and name == stop_after:
                raise _StopEmission()

        def load_w_tiled(dst_ap, src, rows, cols, sem, waits=()):
            nrt = rows // 128
            TOP("gpsimd",
                lambda e, d=dst_ap, s=src, nrt=nrt, cols=cols: e.dma_start(
                    d.rearrange("p (rt c) -> p rt c", rt=nrt),
                    s.rearrange("(rt p) c -> p rt c", p=128)),
                waits=waits, inc=sem, inc_n=16)

        def ln_stats(s, tiles, sq_scr):
            n = len(tiles)
            c = lncol
            for i, t in enumerate(tiles):
                TOP("vector", lambda e, t=t, i=i: e.reduce_sum(
                    c[:, i:i + 1], xt(s, t), axis=AX.X),
                    inc=(sV if i == n - 1 else None))
                TOP("scalar", lambda e, t=t, i=i: e.activation(
                    sq_scr, xt(s, t), AF.Square, accum_out=c[:, 16 + i:17 + i]),
                    inc=(sS if i == n - 1 else None))
            TOP("scalar", lambda e: e.drain())
            TOP("scalar", lambda e: e.nop(), inc=sS)
            P_.wait("vector", [(sS, sS.n)])
            TOP("vector", lambda e: e.drain())
            TOP("vector", lambda e, n=n: e.tensor_scalar(
                c[:, 32:32 + n], c[:, 0:n], 1.0 / D, None, ALU.mult))
            TOP("vector", lambda e, n=n: e.tensor_scalar(
                c[:, 48:48 + n], c[:, 16:16 + n], 1.0 / D, EPS,
                ALU.mult, ALU.add))
            TOP("vector", lambda e, n=n: e.tensor_tensor(
                c[:, 64:64 + n], c[:, 32:32 + n], c[:, 32:32 + n], ALU.mult))
            TOP("vector", lambda e, n=n: e.tensor_scalar(
                c[:, 80:80 + n], c[:, 32:32 + n], -1.0, None, ALU.mult))
            TOP("vector", lambda e, n=n: e.scalar_tensor_tensor(
                c[:, 96:96 + n], c[:, 64:64 + n], -1.0, c[:, 48:48 + n],
                ALU.mult, ALU.add))
            TOP("vector", lambda e: e.drain())
            TOP("vector", lambda e: e.nop(), inc=sV)
            P_.wait("scalar", [(sV, sV.n)])
            TOP("scalar", lambda e, n=n: e.activation(
                c[:, 112:112 + n], c[:, 96:96 + n], AF.Ln))
            TOP("scalar", lambda e: e.drain())
            TOP("scalar", lambda e, n=n: e.activation(
                c[:, 128:128 + n], c[:, 112:112 + n], AF.Exp, scale=-0.5))
            TOP("scalar", lambda e: e.drain())
            TOP("scalar", lambda e: e.nop(), inc=sS)
            P_.wait("vector", [(sS, sS.n)])
            TOP("vector", lambda e, n=n: e.tensor_tensor(
                c[:, 144:144 + n], c[:, 80:80 + n], c[:, 128:128 + n],
                ALU.mult))
            TOP("vector", lambda e: e.drain())
            TOP("vector", lambda e: e.nop(), inc=sV)

        def ln_apply(s, t, i, out_bf, waits=()):
            TOP("scalar", lambda e, t=t, i=i, out_bf=out_bf: e.activation(
                out_bf, xt(s, t), AF.Identity, bias=lncol[:, 144 + i:145 + i],
                scale=lncol[:, 128 + i:129 + i]),
                waits=list(waits) + [(sV, sV.n)], inc=sS)
            return sS.n

        def ln_group(s, tiles, sq_scr, bufs, dest_fn, bank0=6):
            ln_stats(s, tiles, sq_scr)
            apply_inc = []
            tr_done = []
            n = len(tiles)
            for i in range(n + 1):
                if i < n:
                    w = []
                    if i >= 2:
                        w.append((sT, tr_done[i - 2]))
                    apply_inc.append(ln_apply(s, tiles[i], i, bufs[i % 2],
                                              waits=w))
                if i == n:
                    TOP("scalar", lambda e: e.drain())
                    TOP("scalar", lambda e: e.nop(), inc=sS)
                if i >= 1:
                    j = i - 1
                    wv = apply_inc[j] + 1 if j < n - 1 else sS.n
                    transpose_tile(bufs[j % 2], dest_fn(j), bank0 + (j % 2),
                                   [(sS, wv)])
                    tr_done.append(sT.n)

        def transpose_tile(src_bf, dest_strided_ap, bank, src_ready,
                           dummy_bank=5, evac=True):
            pb = P[bank][:].bitcast(BF16)
            for j in range(8):
                TOP("tensor",
                    lambda e, j=j: e.transpose(pb[:, 128 * j:128 * (j + 1)],
                                               src_bf[:, 128 * j:128 * (j + 1)],
                                               ident[:]),
                    waits=((list(src_ready) + [(sV, sV.n)]) if j == 0 else ()))
            TOP("tensor", lambda e: e.drain())
            TOP("tensor", lambda e: e.nop(), inc=sT)
            if evac:
                TOP("vector",
                    lambda e: e.tensor_copy(dest_strided_ap,
                                            pb[:].rearrange("p (j c) -> p j c", j=8)),
                    waits=[(sT, sT.n)], inc=sV)

        Df = Dn_t[:]
        ET = [Dn[:, 512 * i:512 * (i + 1)] for i in range(4)]
        recb = [Df[0:64, 1024 + 256 * (2 * i):1024 + 256 * (2 * i) + 512]
                for i in range(2)]
        rrAB = Df[0:1, 2048:3072]

        def attn_scores_out(qbase, qlen, kbase_fn, ktiles, vbase_fn,
                            masked, ybase_t, entry_waits, wo_ready):
            for hp in range(8):
                pvb = [P[4], P[5]]
                for half in range(2):
                    h = 2 * hp + half
                    r0, r1 = (0, 64) if half == 0 else (64, 128)
                    tp = (0, 0) if half == 0 else (64, 0)
                    sb = [P[2 * half], P[2 * half + 1]]
                    exp_counts = []
                    pv_counts = []
                    for kt in range(ktiles):
                        bank = sb[kt % 2]
                        w0 = list(entry_waits) if (hp == 0 and half == 0
                                                   and kt == 0) else []
                        if kt >= 2:
                            w0.append((sS, exp_counts[kt - 2]))
                        else:
                            w0.append((sS, sS.n))
                        TOP("tensor", lambda e, hp=hp, kt=kt, bank=bank,
                            r0=r0, r1=r1, tp=tp, qbase=qbase, qlen=qlen,
                            kbase_fn=kbase_fn:
                            e.matmul(bank[0:128, 0:qlen],
                                     B[r0:r1, kbase_fn(hp, kt):kbase_fn(hp, kt) + 128],
                                     A[r0:r1, 2048 * hp + qbase:2048 * hp + qbase + qlen],
                                     start=True, stop=True,
                                     tile_position=tp),
                            waits=w0, inc=sT)
                        etb = ET[2 * half + kt % 2]
                        ew = [(sT, sT.n)]
                        if kt >= 2:
                            ew.append((sT, pv_counts[kt - 2]))
                        TOP("scalar", lambda e, bank=bank, etb=etb, qlen=qlen:
                            e.activation(etb[:, 0:qlen], bank[0:128, 0:qlen],
                                         AF.Exp, scale=0.125,
                                         bias=smallf[:, 0:1]),
                            waits=ew, inc=sS)
                        exp_counts.append(sS.n)
                        et_sem = (sS, sS.n)
                        if masked and kt >= ktiles - 2:
                            mt = m0 if kt == ktiles - 2 else m1
                            TOP("vector", lambda e, etb=etb, mt=mt, qlen=qlen:
                                e.tensor_tensor(etb[:, 0:qlen],
                                                etb[:, 0:qlen],
                                                mt[:, 0:qlen], ALU.mult),
                                waits=[(sS, sS.n)], inc=sV)
                            et_sem = (sV, sV.n)
                        pvw = [et_sem]
                        if kt == 0:
                            pvw.append((sV, sV.n))
                        TOP("tensor", lambda e, h=h, kt=kt, half=half,
                            etb=etb, qlen=qlen, vbase_fn=vbase_fn,
                            pv=pvb[half], ktiles=ktiles:
                            e.matmul(pv[0:65, 0:qlen],
                                     C[:, vbase_fn(kt) + 65 * h:vbase_fn(kt) + 65 * h + 65],
                                     etb[:, 0:qlen],
                                     start=(kt == 0), stop=(kt == ktiles - 1)),
                            waits=pvw, inc=sT)
                        pv_counts.append(sT.n)
                    rr = rrAB[0:1, 512 * half:512 * half + qlen]
                    TOP("vector", lambda e, rr=rr, pv=pvb[half], qlen=qlen:
                        e.reciprocal(rr, pv[64:65, 0:qlen]),
                        waits=[(sT, sT.n)], inc=sV)
                    bcb = sb[0]
                    TOP("tensor", lambda e, rr=rr, bcb=bcb, qlen=qlen:
                        e.matmul(bcb[0:64, 0:qlen], ones1[0:1, 0:64], rr,
                                 start=True, stop=True),
                        waits=[(sV, sV.n), (sS, sS.n)], inc=sT)
                    TOP("scalar", lambda e, bcb=bcb, half=half, qlen=qlen:
                        e.copy(recb[half][0:64, 0:qlen], bcb[0:64, 0:qlen]),
                        waits=[(sT, sT.n)], inc=sS)
                    TOP("vector", lambda e, half=half, hp=hp, qlen=qlen,
                        pv=pvb[half], r0=r0, r1=r1:
                        e.tensor_tensor(
                            E[r0:r1, qlen * hp:qlen * hp + qlen],
                            pv[0:64, 0:qlen], recb[half][0:64, 0:qlen],
                            ALU.mult),
                        waits=[(sS, sS.n), (sT, sT.n)], inc=sV)
            y_ready = (sV, sV.n)
            nqi = qlen // 128
            for qi in range(nqi):
                t = ybase_t + qi
                for dn in range(2):
                    bank = P[6 + (qi * 2 + dn) % 2]
                    w0 = [y_ready, (sV, sV.n), wo_ready]
                    for ft in range(8):
                        TOP("tensor", lambda e, qi=qi, dn=dn, ft=ft,
                            bank=bank, qlen=qlen:
                            e.matmul(bank[:, 0:512],
                                     E[:, qlen * ft + 128 * qi:qlen * ft + 128 * qi + 128],
                                     W[:, 1024 * ft + 512 * dn:1024 * ft + 512 * dn + 512],
                                     start=(ft == 0), stop=(ft == 7)),
                            waits=(w0 if ft == 0 else ()),
                            inc=(sT if ft == 7 else None))
                    TOP("vector", lambda e, t=t, dn=dn, bank=bank:
                        e.scalar_tensor_tensor(
                            x_res[:, 1024 * t + 512 * dn:1024 * t + 512 * dn + 512],
                            bank[:, 0:512], 1.0,
                            x_res[:, 1024 * t + 512 * dn:1024 * t + 512 * dn + 512],
                            ALU.mult, ALU.add),
                        waits=[(sT, sT.n)], inc=sV)


        def emit_attn_a(s):
            Df = Dn_t[:]
            Cf = C[:].bitcast(F32)
            LNfm = Dn
            w_ready0 = [(sT, sT.n)]
            if stop_after != "ln_nw" and "now" not in KNOBS:
                load_w_tiled(W[:, 0:4096], dw["aaW1"], D, 512, qGW, waits=w_ready0)
                load_w_tiled(W[:, 4096:8192], dw["aaW2"], 512, D, qGW)
            w12_ready = (qGW, qGW.n)

            sq_scr = Cf[:, 0:1024]
            lntok = [E[:, 0:1024], E[:, 1024:2048]]
            nln = 8
            for kb in KNOBS:
                if kb.startswith("lnt"):
                    nln = int(kb[3:])
            ln_group(s, list(range(nln)), sq_scr, lntok,
                     lambda j: LNfm[:].rearrange("p (k c) -> p k c", k=8)[
                         :, :, 128 * j:128 * (j + 1)])
            lnfm_i_ready = (sV, sV.n)
            if "dumpLN" in KNOBS:
                for ch in range(4):
                    TOP("vector", lambda e, ch=ch: e.tensor_copy(
                        x_res[:, 2048 * ch:2048 * (ch + 1)],
                        Dn[:, 2048 * ch:2048 * (ch + 1)]),
                        waits=[(sV, sV.n), (sS, sS.n)], inc=sV)
                TOP("sync", lambda e: e.dma_start(
                    dbg[0:128, 0:8192].rearrange("p (a b) -> p a b", a=4),
                    x_res[:, 0:8192].rearrange("p (a b) -> p a b", a=4)),
                    waits=[(sV, sV.n)], inc=qSY, inc_n=16)
                raise _StopEmission()
            check_stop("ln0")

            for fm in range(1 if "h1one" in KNOBS else 4):
                for nh in range(1 if "h1one" in KNOBS else 2):
                    bank = P[fm % 2]
                    w0 = [(sS, sS.n), (sV, sV.n), w12_ready, lnfm_i_ready]
                    for dt in range(8):
                        TOP("tensor", lambda e, fm=fm, nh=nh, dt=dt, bank=bank:
                            e.matmul(bank[:, 0:512],
                                     W[:, 512 * dt + 128 * fm:512 * dt + 128 * fm + 128],
                                     LNfm[:, 1024 * dt + 512 * nh:1024 * dt + 512 * nh + 512],
                                     start=(dt == 0), stop=(dt == 7)),
                            waits=(w0 if dt == 0 else ()),
                            inc=(sT if dt == 7 else None))
                    if stop_after != "comp_nm":
                        if "h1vec" in KNOBS:
                            TOP("vector", lambda e, fm=fm, nh=nh, bank=bank:
                                e.tensor_copy(B[:, 1024 * fm + 512 * nh:1024 * fm + 512 * nh + 512],
                                              bank[:, 0:512]),
                                waits=[(sT, sT.n)], inc=sS)
                        else:
                            TOP("scalar", lambda e, fm=fm, nh=nh, bank=bank:
                                e.activation(B[:, 1024 * fm + 512 * nh:1024 * fm + 512 * nh + 512],
                                             bank[:, 0:512], AF.Relu,
                                             bias=bias_a[:, fm:fm + 1]),
                                waits=[(sT, sT.n)], inc=sS)
            Ef32 = E[:].bitcast(F32)
            if stop_after in ("comp_h1", "comp_nm"):
                raise _StopEmission()
            for dt in range(8):
                for nh in range(2):
                    bank = P[2 + ((2 * dt + nh) % 2)]
                    w0 = [(sS, sS.n), (sV, sV.n)]
                    for ft in range(4):
                        TOP("tensor", lambda e, dt=dt, nh=nh, ft=ft, bank=bank:
                            e.matmul(bank[:, 0:512],
                                     W[:, 4096 + 1024 * ft + 128 * dt:4096 + 1024 * ft + 128 * dt + 128],
                                     B[:, 1024 * ft + 512 * nh:1024 * ft + 512 * nh + 512],
                                     start=(ft == 0), stop=(ft == 3)),
                            waits=(w0 if ft == 0 else ()),
                            inc=(sT if ft == 3 else None))
                    scr = Ef32[:, 2048:2560]
                    TOP("scalar", lambda e, dt=dt, bank=bank, scr=scr:
                        e.activation(scr, bank[:, 0:512], AF.Identity,
                                     bias=bias_a[:, 4 + dt:5 + dt]),
                        waits=[(sT, sT.n), (sV, sV.n)], inc=sS)
                    pg = scr.rearrange("p (g f) -> p g f", f=4)
                    pdst = E[:, 2048 + 256 * dt + 128 * nh:2048 + 256 * dt + 128 * nh + 128]
                    psc = Ef32[:, 0:128]
                    if stop_after == "comp_mm":
                        continue
                    TOP("vector", lambda e, pg=pg, psc=psc: e.tensor_tensor(
                        psc, pg[:, :, 0], pg[:, :, 1], ALU.add),
                        waits=[(sS, sS.n)])
                    TOP("vector", lambda e, pg=pg, psc=psc: e.tensor_tensor(
                        psc, psc, pg[:, :, 2], ALU.add))
                    TOP("vector", lambda e, pg=pg, psc=psc: e.tensor_tensor(
                        psc, psc, pg[:, :, 3], ALU.add))
                    TOP("vector", lambda e, psc=psc, pdst=pdst: e.tensor_scalar(
                        pdst, psc, 0.25, None, ALU.mult), inc=sV)
            pooled_ready = (sV, sV.n)
            comp_mm_done = (sT, sT.n)
            if stop_after == "comp_mm":
                raise _StopEmission()
            check_stop("comp")

            load_w_tiled(W[:, 0:8192], dw["aaWq"], D, D, qGW, waits=[comp_mm_done])
            wq_ready = (qGW, qGW.n)
            if "dumpW" in KNOBS:
                P_.wait("vector", [wq_ready])
                for ch in range(4):
                    TOP("vector", lambda e, ch=ch: e.tensor_copy(
                        x_res[:, 2048 * ch:2048 * (ch + 1)],
                        W[:, 2048 * ch:2048 * (ch + 1)]), inc=sV)
                TOP("sync", lambda e: e.dma_start(
                    dbg[0:128, 0:8192].rearrange("p (a b) -> p a b", a=4),
                    x_res[:, 0:8192].rearrange("p (a b) -> p a b", a=4)),
                    waits=[(sV, sV.n)], inc=qSY, inc_n=16)
                raise _StopEmission()
            for ft in range(8):
                for nq in range(2):
                    bank = P[(2 * ft + nq) % 2]
                    w0 = [wq_ready, (sS, sS.n)]
                    for dt in range(8):
                        TOP("tensor", lambda e, ft=ft, nq=nq, dt=dt, bank=bank:
                            e.matmul(bank[:, 0:512],
                                     W[:, 1024 * dt + 128 * ft:1024 * dt + 128 * ft + 128],
                                     LNfm[:, 1024 * dt + 512 * nq:1024 * dt + 512 * nq + 512],
                                     start=(dt == 0), stop=(dt == 7)),
                            waits=(w0 if dt == 0 else ()),
                            inc=(sT if dt == 7 else None))
                    TOP("scalar", lambda e, ft=ft, nq=nq, bank=bank:
                        e.copy(A[:, 2048 * ft + 512 * nq:2048 * ft + 512 * nq + 512],
                               bank[:, 0:512]),
                        waits=[(sT, sT.n)], inc=sS)
            for tc in range(2):
                chunk = B[:, 4096 + 4096 * tc:8192 + 4096 * tc]
                ln_group(s, [8 + 4 * tc + ti for ti in range(4)], sq_scr,
                         lntok,
                         lambda j, chunk=chunk: chunk.rearrange(
                             "p (k c) -> p k c", k=8)[:, :, 128 * j:128 * (j + 1)])
                chunk_ready = (sV, sV.n)
                for ft in range(8):
                    bank = P[ft % 2]
                    w0 = [chunk_ready, (sS, sS.n)]
                    for dt in range(8):
                        TOP("tensor", lambda e, ft=ft, dt=dt, bank=bank, chunk=chunk:
                            e.matmul(bank[:, 0:512],
                                     W[:, 1024 * dt + 128 * ft:1024 * dt + 128 * ft + 128],
                                     chunk[:, 512 * dt:512 * dt + 512],
                                     start=(dt == 0), stop=(dt == 7)),
                            waits=(w0 if dt == 0 else ()),
                            inc=(sT if dt == 7 else None))
                    TOP("scalar", lambda e, ft=ft, tc=tc, bank=bank:
                        e.copy(A[:, 2048 * ft + 1024 + 512 * tc:2048 * ft + 1024 + 512 * tc + 512],
                               bank[:, 0:512]),
                        waits=[(sT, sT.n)], inc=sS)
            q_done_t = (sT, sT.n)
            if "dumpA" in KNOBS:
                for ch in range(8):
                    TOP("vector", lambda e, ch=ch: e.tensor_copy(
                        x_res[:, 2048 * ch:2048 * (ch + 1)],
                        A[:, 2048 * ch:2048 * (ch + 1)]),
                        waits=[(sS, sS.n)], inc=sV)
                TOP("sync", lambda e: e.dma_start(
                    dbg[:].rearrange("p (a b) -> p a b", a=8),
                    x_res[:].rearrange("p (a b) -> p a b", a=8)),
                    waits=[(sV, sV.n)], inc=qSY, inc_n=16)
                raise _StopEmission()
            check_stop("q")

            load_w_tiled(W[:, 0:8192], dw["aaWk"], D, D, qGW, waits=[q_done_t])
            wk_ready = (qGW, qGW.n)
            for ft in range(8):
                for nk in range(2):
                    bank = P[(2 * ft + nk) % 2]
                    w0 = [wk_ready, (sS, sS.n)]
                    for dt in range(8):
                        TOP("tensor", lambda e, ft=ft, nk=nk, dt=dt, bank=bank:
                            e.matmul(bank[:, 0:512],
                                     W[:, 1024 * dt + 128 * ft:1024 * dt + 128 * ft + 128],
                                     LNfm[:, 1024 * dt + 512 * nk:1024 * dt + 512 * nk + 512],
                                     start=(dt == 0), stop=(dt == 7)),
                            waits=(w0 if dt == 0 else ()),
                            inc=(sT if dt == 7 else None))
                    TOP("scalar", lambda e, ft=ft, nk=nk, bank=bank:
                        e.copy(B[:, 1024 * ft + 512 * nk:1024 * ft + 512 * nk + 512],
                               bank[:, 0:512]),
                        waits=[(sT, sT.n)], inc=sS)
            for ft in range(8):
                bank = P[ft % 2]
                w0 = [wk_ready, pooled_ready, (sS, sS.n)]
                for dt in range(8):
                    TOP("tensor", lambda e, ft=ft, dt=dt, bank=bank:
                        e.matmul(bank[:, 0:256],
                                 W[:, 1024 * dt + 128 * ft:1024 * dt + 128 * ft + 128],
                                 E[:, 2048 + 256 * dt:2048 + 256 * dt + 256],
                                 start=(dt == 0), stop=(dt == 7)),
                        waits=(w0 if dt == 0 else ()),
                        inc=(sT if dt == 7 else None))
                TOP("scalar", lambda e, ft=ft, bank=bank:
                    e.copy(B[:, 8192 + 256 * ft:8192 + 256 * ft + 256],
                           bank[:, 0:256]),
                    waits=[(sT, sT.n)], inc=sS)
            k_done_t = (sT, sT.n)
            check_stop("k")

            load_w_tiled(W[:, 0:8192], dw["aaWv"], D, D, qGW, waits=[k_done_t])
            wv_ready = (qGW, qGW.n)
            TOP("vector", lambda e: e.memset(
                C[:, 0:10400].rearrange("p (h c) -> p h c", c=65)[:, :, 64:65],
                1.0), inc=sV)
            for tt in range(8):
                for nv in range(2):
                    bank = P[(2 * tt + nv) % 2]
                    w0 = [wv_ready, (sV, sV.n)]
                    for dt in range(8):
                        TOP("tensor", lambda e, tt=tt, nv=nv, dt=dt, bank=bank:
                            e.matmul(bank[:, 0:512],
                                     LNfm[:, 1024 * dt + 128 * tt:1024 * dt + 128 * tt + 128],
                                     W[:, 1024 * dt + 512 * nv:1024 * dt + 512 * nv + 512],
                                     start=(dt == 0), stop=(dt == 7)),
                            waits=(w0 if dt == 0 else ()),
                            inc=(sT if dt == 7 else None))
                    dst = C[:, 1040 * tt + 520 * nv:1040 * tt + 520 * nv + 520]
                    TOP("vector", lambda e, dst=dst, bank=bank: e.tensor_copy(
                        dst.rearrange("p (h c) -> p h c", c=65)[:, :, 0:64],
                        bank[:, 0:512].rearrange("p (h c) -> p h c", c=64)),
                        waits=[(sT, sT.n)], inc=sV)
            for ct in range(2):
                for nv in range(2):
                    bank = P[(2 * ct + nv) % 2]
                    w0 = [(sV, sV.n)]
                    for dt in range(8):
                        TOP("tensor", lambda e, ct=ct, nv=nv, dt=dt, bank=bank:
                            e.matmul(bank[:, 0:512],
                                     E[:, 2048 + 256 * dt + 128 * ct:2048 + 256 * dt + 128 * ct + 128],
                                     W[:, 1024 * dt + 512 * nv:1024 * dt + 512 * nv + 512],
                                     start=(dt == 0), stop=(dt == 7)),
                            waits=(w0 if dt == 0 else ()),
                            inc=(sT if dt == 7 else None))
                    dst = C[:, 8320 + 1040 * ct + 520 * nv:8320 + 1040 * ct + 520 * nv + 520]
                    TOP("vector", lambda e, dst=dst, bank=bank: e.tensor_copy(
                        dst.rearrange("p (h c) -> p h c", c=65)[:, :, 0:64],
                        bank[:, 0:512].rearrange("p (h c) -> p h c", c=64)),
                        waits=[(sT, sT.n)], inc=sV)
            v_done_t = (sT, sT.n)
            check_stop("v")

            load_w_tiled(W[:, 0:8192], dw["aaWo"], D, D, qGW, waits=[v_done_t])
            wo_ready = (qGW, qGW.n)
            entry = [(sS, sS.n), (sV, sV.n), (sT, sT.n)]
            for qc in range(4):
                attn_scores_out(
                    qbase=256 * qc, qlen=256,
                    kbase_fn=lambda hp, kt: 1024 * hp + 128 * kt,
                    ktiles=2 * qc + 2,
                    vbase_fn=lambda kt: 1040 * kt,
                    masked=True, ybase_t=2 * qc, entry_waits=entry,
                    wo_ready=wo_ready)
                entry = []
                check_stop(f"causal{qc}")
            for qc2 in range(2):
                attn_scores_out(
                    qbase=1024 + 512 * qc2, qlen=512,
                    kbase_fn=lambda hp, kt: 8192 + 256 * hp + 128 * kt,
                    ktiles=2,
                    vbase_fn=lambda kt: 8320 + 1040 * kt,
                    masked=False, ybase_t=8 + 4 * qc2, entry_waits=[],
                    wo_ready=wo_ready)

        def emit_ffn(s):
            Cf = C[:].bitcast(F32)
            Wf = W[:].bitcast(F32)
            Df = Dn_t[:]
            iota_row = Wf[:, 512:1024]
            probsT = Cf[0:8, 0:2048]
            ones8 = Cf[0:8, 2048:4096]
            scanw = Cf[0:8, 4096:6144]
            identf = Wf[:, 1280:1408]
            lnfmt = [E[:, 2048:3072], E[:, 3072:4096]]
            lntok = [E[:, 0:1024], E[:, 1024:2048]]

            TOP("gpsimd", lambda e: e.iota(iota_row, pattern=[[1, 512]], base=0,
                                           channel_multiplier=0,
                                           allow_small_or_imprecise_dtypes=True),
                waits=[(sT, sT.n)], inc=sG)
            TOP("sync", lambda e: e.dma_start(identf, c_identf[:]),
                inc=qSY, inc_n=16)
            P_.wait("tensor", [(qSY, qSY.n), (sG, sG.n)])
            TOP("sync", lambda e: e.dma_start(
                d_acc.rearrange("(t p) d -> p t d", p=128),
                x_res[:].rearrange("p (t d) -> p t d", t=NT)),
                inc=qSY, inc_n=16)
            acc_init_q = qSY.n

            ln_stats(s, list(range(16)), Df[:, 0:1024])
            apply_inc = []
            tr_done = []
            gate_evac = []
            dma_q = []
            for i in range(17):
                if i < 16:
                    w = []
                    if i >= 2:
                        w.append((sT, tr_done[i - 2]))
                        w.append((qSY, dma_q[i - 2]))
                    apply_inc.append(ln_apply(s, i, i, lntok[i % 2], waits=w))
                if i == 16:
                    TOP("scalar", lambda e: e.drain())
                    TOP("scalar", lambda e: e.nop(), inc=sS)
                if i >= 1:
                    j = i - 1
                    wv = apply_inc[j] + 1 if j < 15 else sS.n
                    buf = lntok[j % 2]
                    TOP("sync", lambda e, j=j, buf=buf: e.dma_start(
                        d_ln[128 * j:128 * (j + 1), :], buf),
                        waits=[(sS, wv)], inc=qSY, inc_n=16)
                    dma_q.append(qSY.n)
                    transpose_tile(buf, lnfmt[j % 2][:].rearrange(
                        "p (k c) -> p k c", k=8), 6 + (j % 2), [(sS, wv)])
                    tr_done.append(sT.n)
                    gw0 = [(sV, sV.n)]
                    if j >= 1:
                        gw0.append((sS, gate_evac[j - 1]))
                    for dt in range(8):
                        TOP("tensor", lambda e, j=j, dt=dt: e.matmul(
                            P[4][:, 0:8],
                            lnfmt[j % 2][:, 128 * dt:128 * (dt + 1)],
                            gateT[:, 8 * dt:8 * (dt + 1)],
                            start=(dt == 0), stop=(dt == 7)),
                            waits=(gw0 if dt == 0 else ()),
                            inc=(sT if dt == 7 else None))
                    TOP("scalar", lambda e, j=j: e.activation(
                        probs[:, 8 * j:8 * (j + 1)], P[4][:, 0:8], AF.Exp,
                        bias=smallf[:, 0:1]),
                        waits=[(sT, sT.n)], inc=sS)
                    gate_evac.append(sS.n)
            P_.wait("vector", [(sS, sS.n)])
            TOP("vector", lambda e: e.drain())
            for t in range(16):
                TOP("vector", lambda e, t=t: e.reduce_sum(
                    smallf[:, 8 + t:9 + t], probs[:, 8 * t:8 * (t + 1)],
                    axis=AX.X))
            TOP("vector", lambda e: e.drain())
            TOP("vector", lambda e: e.reciprocal(smallf[:, 24:25],
                                                 smallf[:, 8:9]))
            for t in range(16):
                TOP("vector", lambda e, t=t: e.reciprocal(
                    smallf[:, 8 + t:9 + t], smallf[:, 8 + t:9 + t]))
            TOP("vector", lambda e: e.drain())
            for t in range(16):
                TOP("vector", lambda e, t=t: e.tensor_scalar(
                    probs[:, 8 * t:8 * (t + 1)], probs[:, 8 * t:8 * (t + 1)],
                    smallf[:, 8 + t:9 + t], None, ALU.mult),
                    inc=(sV if t == 15 else None))
            TOP("sync", lambda e: e.dma_start(
                d_probs.rearrange("(t p) e -> p t e", p=128),
                probs[:].rearrange("p (t e) -> p t e", t=16)),
                waits=[(sV, sV.n)], inc=qSY, inc_n=16)
            probs_q = qSY.n

            for g4 in range(4):
                for tt in range(4):
                    t = 4 * g4 + tt
                    TOP("tensor", lambda e, t=t, tt=tt, g4=g4: e.transpose(
                        P[6 + (g4 % 2)][0:8, 128 * tt:128 * (tt + 1)],
                        probs[:, 8 * t:8 * (t + 1)], identf[0:128, 0:128]),
                        waits=([(sV, sV.n)] if tt == 0 else ()))
                TOP("tensor", lambda e: e.drain())
                TOP("tensor", lambda e: e.nop(), inc=sT)
                TOP("vector", lambda e, g4=g4: e.tensor_copy(
                    probsT[:, 512 * g4:512 * (g4 + 1)],
                    P[6 + (g4 % 2)][0:8, 0:512]),
                    waits=[(sT, sT.n)], inc=sV)

            b = bis
            TOP("vector", lambda e: e.memset(b[0:8, 0:1], 0.0))
            TOP("vector", lambda e: e.memset(b[0:8, 1:2], 1.0))
            TOP("vector", lambda e: e.memset(b[0:8, 2:3], 0.5))
            TOP("vector", lambda e: e.drain())
            TOP("vector", lambda e: e.nop(), inc=sV)
            for it in range(BISECT_ITERS):
                TOP("vector", lambda e: e.tensor_scalar(
                    scanw, probsT, b[0:8, 2:3], None, ALU.is_ge))
                TOP("vector", lambda e: e.reduce_sum(b[0:8, 5:6], scanw,
                                                     axis=AX.X))
                TOP("vector", lambda e: e.tensor_tensor(
                    b[0:8, 3:4], b[0:8, 1:2], b[0:8, 0:1], ALU.subtract))
                TOP("vector", lambda e: e.drain())
                TOP("vector", lambda e: e.tensor_scalar(
                    b[0:8, 6:7], b[0:8, 5:6], 512.5, None, ALU.is_ge))
                TOP("vector", lambda e: e.tensor_scalar(
                    b[0:8, 4:5], b[0:8, 3:4], 0.5, None, ALU.mult))
                TOP("vector", lambda e: e.drain())
                TOP("vector", lambda e: e.tensor_scalar(
                    b[0:8, 7:8], b[0:8, 6:7], -1.0, 1.0, ALU.mult, ALU.add))
                TOP("vector", lambda e: e.tensor_tensor(
                    b[0:8, 3:4], b[0:8, 4:5], b[0:8, 6:7], ALU.mult))
                TOP("vector", lambda e: e.drain())
                TOP("vector", lambda e: e.tensor_tensor(
                    b[0:8, 0:1], b[0:8, 0:1], b[0:8, 3:4], ALU.add))
                TOP("vector", lambda e: e.tensor_tensor(
                    b[0:8, 3:4], b[0:8, 4:5], b[0:8, 7:8], ALU.mult))
                TOP("vector", lambda e: e.drain())
                TOP("vector", lambda e: e.tensor_tensor(
                    b[0:8, 1:2], b[0:8, 1:2], b[0:8, 3:4], ALU.subtract))
                TOP("vector", lambda e: e.drain())
                TOP("vector", lambda e: e.tensor_tensor(
                    b[0:8, 3:4], b[0:8, 1:2], b[0:8, 0:1], ALU.subtract))
                TOP("vector", lambda e: e.drain())
                TOP("vector", lambda e: e.scalar_tensor_tensor(
                    b[0:8, 2:3], b[0:8, 3:4], 0.5, b[0:8, 0:1],
                    ALU.mult, ALU.add))
                TOP("vector", lambda e: e.drain())
            TOP("vector", lambda e: e.nop(), inc=sV)

            TOP("vector", lambda e: e.memset(ones8, 1.0))
            TOP("vector", lambda e: e.tensor_scalar(
                scanw, probsT, b[0:8, 1:2], None, ALU.is_ge))
            TOP("vector", lambda e: e.drain())
            TOP("vector", lambda e: e.tensor_tensor_scan(
                Cf[0:8, 6144:8192], scanw, ones8, 0.0, ALU.add, ALU.mult))
            TOP("vector", lambda e: e.drain())
            TOP("vector", lambda e: e.tensor_tensor(
                Cf[0:8, 6144:8192], Cf[0:8, 6144:8192], scanw, ALU.mult))
            TOP("vector", lambda e: e.tensor_scalar(
                scanw, scanw, -600.0, 599.0, ALU.mult, ALU.add))
            TOP("vector", lambda e: e.drain())
            TOP("vector", lambda e: e.tensor_tensor(
                Cf[0:8, 6144:8192], Cf[0:8, 6144:8192], scanw, ALU.add),
                inc=sV)
            for g4 in range(4):
                for tt in range(4):
                    t = 4 * g4 + tt
                    TOP("tensor", lambda e, t=t, tt=tt, g4=g4: e.transpose(
                        P[6 + (g4 % 2)][:, 8 * tt:8 * tt + 8],
                        Cf[0:8, 6144 + 128 * t:6144 + 128 * (t + 1)],
                        identf[0:8, 0:8]),
                        waits=([(sV, sV.n)] if tt == 0 else ()))
                TOP("tensor", lambda e: e.drain())
                TOP("tensor", lambda e: e.nop(), inc=sT)
                TOP("vector", lambda e, g4=g4: e.tensor_copy(
                    rankt[:, 32 * g4:32 * (g4 + 1)],
                    P[6 + (g4 % 2)][:, 0:32]),
                    waits=[(sT, sT.n)], inc=sV)

            cp_done = []
            for e_ in range(NE):
                for t in range(16):
                    w = [(sV, sV.n)] if t == 0 else []
                    if t >= 2 or e_ > 0:
                        w.append((sT, sT.n))
                    TOP("vector", lambda e, e_=e_, t=t: e.tensor_scalar(
                        W[:, 512 * (t % 2):512 * (t % 2) + 512],
                        iota_row, rankt[:, 8 * t + e_:8 * t + e_ + 1], None,
                        ALU.is_equal),
                        waits=w, inc=sV)
                    for sc in range(4):
                        TOP("tensor", lambda e, t=t, sc=sc: e.matmul(
                            P[sc][:, 0:2],
                            W[:, 512 * (t % 2) + 128 * sc:512 * (t % 2) + 128 * (sc + 1)],
                            idxp[:, 2 * t:2 * (t + 1)],
                            start=(t == 0), stop=(t == 15)),
                            waits=([(sV, sV.n)] if sc == 0 else ()),
                            inc=(sT if sc == 3 else None))
                TOP("tensor", lambda e: e.drain())
                TOP("tensor", lambda e: e.nop(), inc=sT)
                for sc in range(4):
                    TOP("scalar", lambda e, sc=sc: e.copy(
                        idxf[:, 2 * sc:2 * sc + 2], P[sc][:, 0:2]),
                        waits=([(sT, sT.n)] if sc == 0 else ()),
                        inc=(sS if sc == 3 else None))
                TOP("scalar", lambda e: e.drain())
                TOP("scalar", lambda e: e.nop(), inc=sS)
                for sc in range(4):
                    TOP("vector", lambda e, e_=e_, sc=sc: e.scalar_tensor_tensor(
                        smallf[:, 16 + sc:17 + sc], idxf[:, 2 * sc + 1:2 * sc + 2],
                        128.0, idxf[:, 2 * sc:2 * sc + 1], ALU.mult, ALU.add),
                        waits=([(sS, sS.n)] if sc == 0 else ()))
                for sc in range(4):
                    TOP("vector", lambda e, e_=e_, sc=sc: e.tensor_copy(
                        idxall[:, 4 * e_ + sc:4 * e_ + sc + 1],
                        smallf[:, 16 + sc:17 + sc]),
                        inc=(sV if sc == 3 else None))
                cp_done.append(sV.n)
            TOP("vector", lambda e: e.drain())
            TOP("vector", lambda e: e.nop(), inc=sV)
            idx_ready = (sV, sV.n)

            for e_ in range(NE):
                for sc in range(4):
                    TOP("gpsimd", lambda e, e_=e_, sc=sc: e.indirect_dma_start(
                        gall[:, 8 * (4 * e_ + sc):8 * (4 * e_ + sc) + 8], None,
                        d_probs[:], bass.IndirectOffsetOnAxis(
                            idxall[:, 4 * e_ + sc:4 * e_ + sc + 1], 0)),
                        waits=([idx_ready, (qSY, probs_q)]
                               if e_ == 0 and sc == 0 else ()),
                        inc=qGM, inc_n=16)
            g_ready = (qGM, qGM.n)

            h1_evac = {}
            for e_ in range(NE):
                xw = [(qSY, dma_q[-1])]
                if e_ == 0:
                    xw.append(idx_ready)
                else:
                    xw.append((sT, sT.n))
                for sc in range(4):
                    TOP("gpsimd", lambda e, e_=e_, sc=sc: e.indirect_dma_start(
                        C[:, 1024 * sc:1024 * (sc + 1)], None, d_ln[:],
                        bass.IndirectOffsetOnAxis(
                            idxall[:, 4 * e_ + sc:4 * e_ + sc + 1], 0)),
                        waits=(xw if sc == 0 else ()),
                        inc=qXI[e_ % 2], inc_n=16)
                for sc in range(4):
                    transpose_tile(
                        C[:, 1024 * sc:1024 * (sc + 1)],
                        C[:, 4096:8192].rearrange("p (k c) -> p k c", k=8)[
                            :, :, 128 * sc:128 * (sc + 1)],
                        6 + (sc % 2),
                        [(qXI[e_ % 2], qXI[e_ % 2].n)] if sc == 0 else [])
                xin_ready = (sV, sV.n)
                for fq in range(4):
                    qb = fq % 2
                    w = [(sS, h1_evac.get((e_, fq - 2),
                                          h1_evac.get((e_ - 1, fq + 2), 0)))]
                    TOP("gpsimd", lambda e, e_=e_, fq=fq, qb=qb: e.dma_start(
                        A[:, 8192 * qb:8192 * (qb + 1)].rearrange(
                            "p (rt c) -> p rt c", rt=8),
                        dw["w1"][e_].rearrange("(rt p) c -> p rt c", p=128)[
                            :, :, 1024 * fq:1024 * (fq + 1)]),
                        waits=w, inc=qW1[qb], inc_n=16)
                    w1q_ready = (qW1[qb], qW1[qb].n)
                    for fl in range(8):
                        bank = P[fl % 2]
                        gw = [w1q_ready, xin_ready, (sS, sS.n)]
                        for dt in range(8):
                            TOP("tensor", lambda e, qb=qb, fl=fl, dt=dt,
                                bank=bank: e.matmul(
                                    bank[:, 0:512],
                                    A[:, 8192 * qb + 1024 * dt + 128 * fl:
                                      8192 * qb + 1024 * dt + 128 * (fl + 1)],
                                    C[:, 4096 + 512 * dt:4096 + 512 * (dt + 1)],
                                    start=(dt == 0), stop=(dt == 7)),
                                waits=(gw if dt == 0 else ()),
                                inc=(sT if dt == 7 else None))
                        TOP("scalar", lambda e, fq=fq, fl=fl, bank=bank:
                            e.activation(
                                B[:, 512 * (8 * fq + fl):512 * (8 * fq + fl) + 512],
                                bank[:, 0:512], AF.Gelu),
                            waits=[(sT, sT.n)], inc=sS)
                    h1_evac[(e_, fq)] = sS.n
                h2w0 = [(sS, sS.n), (sV, sV.n)]
                for ft in range(32):
                    tb = ft % 2
                    TOP("gpsimd", lambda e, e_=e_, ft=ft, tb=tb: e.dma_start(
                        C[:, 8192 + 1024 * tb:8192 + 1024 * (tb + 1)],
                        dw["w2"][e_][128 * ft:128 * (ft + 1), :]),
                        waits=([(sT, sT.n)] if ft >= 2 else []),
                        inc=qW2[tb], inc_n=16)
                    w2t_ready = (qW2[tb], qW2[tb].n)
                    for sm in range(4):
                        for dn in range(2):
                            bank = P[2 * sm + dn]
                            if ft == 0 and sm == 0 and dn == 0:
                                w = list(h2w0) + [w2t_ready]
                            elif sm == 0 and dn == 0:
                                w = [w2t_ready]
                            else:
                                w = []
                            TOP("tensor", lambda e, ft=ft, sm=sm, dn=dn,
                                tb=tb, bank=bank: e.matmul(
                                    bank[:, 0:512],
                                    B[:, 512 * ft + 128 * sm:512 * ft + 128 * (sm + 1)],
                                    C[:, 8192 + 1024 * tb + 512 * dn:
                                      8192 + 1024 * tb + 512 * (dn + 1)],
                                    start=(ft == 0), stop=(ft == 31)),
                                waits=w,
                                inc=(sT if (sm == 3 and dn == 1) else None))
                ev = []
                for sm in range(4):
                    for dn in range(2):
                        bank = P[2 * sm + dn]
                        w = [(sT, sT.n)]
                        if e_ > 0:
                            w.append((qGS, 64 * e_))
                        TOP("vector", lambda e, e_=e_, sm=sm, dn=dn, bank=bank:
                            e.tensor_scalar(
                                Df[:, 1024 * sm + 512 * dn:1024 * sm + 512 * (dn + 1)],
                                bank[:, 0:512],
                                gall[:, 8 * (4 * e_ + sm) + e_:8 * (4 * e_ + sm) + e_ + 1],
                                None, ALU.mult),
                            waits=(w + [g_ready]) if (sm == 0 and dn == 0) else [],
                            inc=sV)
                ev_done = sV.n
                TOP("vector", lambda e: e.drain())
                TOP("vector", lambda e: e.nop(), inc=sV)
                for sm in range(4):
                    TOP("gpsimd", lambda e, e_=e_, sm=sm: e.indirect_dma_start(
                        d_acc[:], bass.IndirectOffsetOnAxis(
                            idxall[:, 4 * e_ + sm:4 * e_ + sm + 1], 0),
                        Df[:, 1024 * sm:1024 * (sm + 1)], None,
                        compute_op=ALU.add, oob_is_err=True),
                        waits=([(sV, sV.n), (qGS, 64 * e_),
                                (qSY, acc_init_q)] if sm == 0 else ()),
                        inc=qGS, inc_n=16)
            TOP("sync", lambda e: e.dma_start(
                x_res[:].rearrange("p (t d) -> p t d", t=NT),
                d_acc.rearrange("(t p) d -> p t d", p=128)),
                waits=[(qGS, qGS.n)], inc=qSY, inc_n=16)
            P_.wait("sync", [(qSY, qSY.n)])

        def emit_attn_b(s):
            sq_scr = C[:].bitcast(F32)[:, 1024:2048]
            lntok = [E[:, 0:1024], E[:, 1024:2048]]
            lntok = [B[:, 0:1024], B[:, 1024:2048]]
            load_w_tiled(W[:, 0:8192], dw["abWq"], D, D, qGW,
                         waits=[(sT, sT.n)])
            load_w_tiled(C[:, 0:2048], dw["abWl"], D, 256, qGW)
            wql_ready = (qGW, qGW.n)
            for tc in range(4):
                chunk = Dn[:, 4096 * (tc % 2):4096 * (tc % 2) + 4096]
                ln_group(s, [4 * tc + ti for ti in range(4)], sq_scr, lntok,
                         lambda j, chunk=chunk: chunk.rearrange(
                             "p (k c) -> p k c", k=8)[:, :, 128 * j:128 * (j + 1)])
                chunk_ready = (sV, sV.n)
                for ft in range(8):
                    bank = P[ft % 2]
                    w0 = [chunk_ready, wql_ready, (sS, sS.n)]
                    for dt in range(8):
                        TOP("tensor", lambda e, ft=ft, dt=dt, bank=bank,
                            chunk=chunk: e.matmul(
                                bank[:, 0:512],
                                W[:, 1024 * dt + 128 * ft:1024 * dt + 128 * (ft + 1)],
                                chunk[:, 512 * dt:512 * (dt + 1)],
                                start=(dt == 0), stop=(dt == 7)),
                            waits=(w0 if dt == 0 else ()),
                            inc=(sT if dt == 7 else None))
                    TOP("scalar", lambda e, ft=ft, tc=tc, bank=bank:
                        e.activation(
                            A[:, 2048 * ft + 512 * tc:2048 * ft + 512 * tc + 512],
                            bank[:, 0:512], AF.Identity,
                            bias=bias_b[:, ft:ft + 1]),
                        waits=[(sT, sT.n)], inc=sS)
                for lt in range(2):
                    bank = P[2 + lt]
                    w0 = [(sS, sS.n)]
                    for dt in range(8):
                        TOP("tensor", lambda e, lt=lt, dt=dt, bank=bank,
                            chunk=chunk: e.matmul(
                                bank[:, 0:512],
                                C[:, 256 * dt + 128 * lt:256 * dt + 128 * (lt + 1)],
                                chunk[:, 512 * dt:512 * (dt + 1)],
                                start=(dt == 0), stop=(dt == 7)),
                            waits=(w0 if dt == 0 else ()),
                            inc=(sT if dt == 7 else None))
                    TOP("scalar", lambda e, lt=lt, tc=tc, bank=bank:
                        e.activation(
                            E[:, 2048 * lt + 512 * tc:2048 * lt + 512 * tc + 512],
                            bank[:, 0:512], AF.Identity,
                            bias=bias_b[:, 8 + lt:9 + lt]),
                        waits=[(sT, sT.n)], inc=sS)
            q_lat_t = (sT, sT.n)

            load_w_tiled(W[:, 0:2048], dw["abWk"], 256, D, qGW,
                         waits=[q_lat_t])
            load_w_tiled(W[:, 2048:4096], dw["abWv"], 256, D, qGW)
            wkv_ready = (qGW, qGW.n)
            TOP("sync", lambda e: e.dma_start(
                Df[0:1, 3072:4096], dw["abbv"].rearrange("(o d) -> o d", o=1)),
                waits=[(sV, sV.n), (sS, sS.n)], inc=qSY, inc_n=16)
            TOP("sync", lambda e: e.dma_start(
                Df[0:1, 2048:3072], dw["abbo"].rearrange("(o d) -> o d", o=1)),
                inc=qSY, inc_n=16)
            brow_q = qSY.n
            for half, (dst0, src0) in enumerate([(0, 3072), (1024, 2048)]):
                for nn in range(2):
                    TOP("tensor", lambda e, src0=src0, nn=nn: e.matmul(
                        P[4][:, 0:512], ones1[0:1, 0:128],
                        Df[0:1, src0 + 512 * nn:src0 + 512 * (nn + 1)],
                        start=True, stop=True),
                        waits=[(qSY, brow_q), (sV, sV.n), (sS, sS.n)],
                        inc=sT)
                    TOP("scalar", lambda e, dst0=dst0, nn=nn: e.copy(
                        Df[:, dst0 + 512 * nn:dst0 + 512 * (nn + 1)],
                        P[4][:, 0:512]),
                        waits=[(sT, sT.n)], inc=sS)
            bvb_ready = (sS, sS.n)
            P_.wait("vector", [bvb_ready])
            TOP("vector", lambda e: e.drain())
            for t in range(NT):
                TOP("vector", lambda e, t=t: e.tensor_tensor(
                    x_res[:, 1024 * t:1024 * (t + 1)],
                    x_res[:, 1024 * t:1024 * (t + 1)],
                    Df[:, 1024:2048], ALU.add),
                    inc=(sV if t == NT - 1 else None))
            TOP("vector", lambda e: e.memset(
                C[:, 0:16640].rearrange("p (h c) -> p h c", c=65)[:, :, 64:65],
                1.0), inc=sV)
            for ft in range(8):
                for nk in range(4):
                    bank = P[(2 * ft + nk) % 2]
                    w0 = [wkv_ready, (sS, sS.n)]
                    for lt in range(2):
                        TOP("tensor", lambda e, ft=ft, nk=nk, lt=lt, bank=bank:
                            e.matmul(
                                bank[:, 0:512],
                                W[:, 1024 * lt + 128 * ft:1024 * lt + 128 * (ft + 1)],
                                E[:, 2048 * lt + 512 * nk:2048 * lt + 512 * (nk + 1)],
                                start=(lt == 0), stop=(lt == 1)),
                            waits=(w0 if lt == 0 else ()),
                            inc=(sT if lt == 1 else None))
                    TOP("scalar", lambda e, ft=ft, nk=nk, bank=bank:
                        e.activation(
                            B[:, 2048 * ft + 512 * nk:2048 * ft + 512 * nk + 512],
                            bank[:, 0:512], AF.Identity,
                            bias=bias_b[:, 10 + ft:11 + ft]),
                        waits=[(sT, sT.n)], inc=sS)
            for tt in range(16):
                for nv in range(2):
                    bank = P[2 + (2 * tt + nv) % 2]
                    w0 = [(sV, sV.n)]
                    for lt in range(2):
                        TOP("tensor", lambda e, tt=tt, nv=nv, lt=lt, bank=bank:
                            e.matmul(
                                bank[:, 0:512],
                                E[:, 2048 * lt + 128 * tt:2048 * lt + 128 * (tt + 1)],
                                W[:, 2048 + 1024 * lt + 512 * nv:2048 + 1024 * lt + 512 * (nv + 1)],
                                start=(lt == 0), stop=(lt == 1)),
                            waits=(w0 if lt == 0 else ()),
                            inc=(sT if lt == 1 else None))
                    dst = C[:, 1040 * tt + 520 * nv:1040 * tt + 520 * nv + 520]
                    TOP("vector", lambda e, dst=dst, bank=bank, nv=nv:
                        e.scalar_tensor_tensor(
                            dst.rearrange("p (h c) -> p h c", c=65)[:, :, 0:64],
                            bank[:, 0:512].rearrange("p (h c) -> p h c", c=64),
                            1.0,
                            Df[:, 512 * nv:512 * (nv + 1)].rearrange(
                                "p (h c) -> p h c", c=64),
                            ALU.mult, ALU.add),
                        waits=[(sT, sT.n)], inc=sV)
            v_done_t = (sT, sT.n)

            load_w_tiled(W[:, 0:8192], dw["abWo"], D, D, qGW,
                         waits=[v_done_t])
            wo_ready = (qGW, qGW.n)
            entry = [(sS, sS.n), (sV, sV.n), (sT, sT.n)]
            for qc in range(4):
                attn_scores_out(
                    qbase=512 * qc, qlen=512,
                    kbase_fn=lambda hp, kt: 2048 * hp + 128 * kt,
                    ktiles=16,
                    vbase_fn=lambda kt: 1040 * kt,
                    masked=False, ybase_t=4 * qc, entry_waits=entry,
                    wo_ready=wo_ready)
                entry = []

        TOP("sync", lambda e: e.dma_start(ident[:], c_ident[:]), inc=qSY, inc_n=16)
        if stop_after != "c1b":
            TOP("sync", lambda e: e.dma_start(m0[:], c_m0[:]), inc=qSY, inc_n=16)
            TOP("sync", lambda e: e.dma_start(m1[:], c_m1[:]), inc=qSY, inc_n=16)
            TOP("sync", lambda e: e.dma_start(idxp[:], c_idxp[:]), inc=qSY, inc_n=16)
        g8 = E[0:8, 0:1024]
        TOP("gpsimd", lambda e: e.dma_start(g8, dw["gw"][:]), inc=qGW, inc_n=16)
        for dst, srcname in ([] if stop_after == "c1b" else [
                             (bias_a[:, 0:4], "aab1"), (bias_a[:, 4:12], "aab2"),
                             (bias_b[:, 0:8], "abbq"), (bias_b[:, 8:10], "abbl"),
                             (bias_b[:, 10:18], "abbk")]):
            TOP("sync", lambda e, dst=dst, srcname=srcname: e.dma_start(
                dst, dw[srcname].rearrange("(a p) -> p a", p=128),
                allow_slow_non_contiguous=True), inc=qSY, inc_n=16)
        TOP("vector", lambda e: e.memset(ones1[:], 1.0), inc=sV)
        TOP("vector", lambda e: e.memset(smallf[:, 0:1], -SHIFT), inc=sV)
        skip_gatet = stop_after in ("c0", "setup_nt") or "ngt" in KNOBS
        skip_gatet_evac = stop_after in ("c0", "c1a", "setup_ne") or "nge" in KNOBS
        skip_extra_consts = stop_after == "c1b"
        gate_ready_w = [(qGW, qGW.n), (qSY, qSY.n)]
        for j in range(8 if not skip_gatet else 0):
            TOP("tensor",
                lambda e, j=j: e.transpose(
                    P[0][:].bitcast(BF16)[:, 8 * j:8 * (j + 1)],
                    g8[:, 128 * j:128 * (j + 1)], ident[0:8, 0:8]),
                waits=(gate_ready_w if j == 0 else ()))
        if not skip_gatet:
            TOP("tensor", lambda e: e.drain())
            TOP("tensor", lambda e: e.nop(), inc=sT)
        if not skip_gatet and not skip_gatet_evac:
            if stop_after == "c1f":
                TOP("tensor", lambda e: e.matmul(P[1][0:8, 0:8], ident[0:8, 0:8],
                                                 ident[0:8, 0:8], start=True,
                                                 stop=True), inc=sT)
            ev_eng = "scalar" if stop_after == "c1e" else "vector"
            if ev_eng == "scalar":
                TOP("scalar", lambda e: e.copy(gateT[:],
                                               P[0][:].bitcast(BF16)[:, 0:64]),
                    waits=[(sT, sT.n)], inc=sS)
            else:
                TOP("vector", lambda e: e.tensor_copy(gateT[:],
                                                      P[0][:].bitcast(BF16)[:, 0:64]),
                    waits=[(sT, sT.n)], inc=sV)
        if stop_after not in ("c1", "c1a", "c1b", "c1e", "c1f", "setup_nb1",
                              "setup_nball"):
            barrier()

        for s in range(2):
            TOP("sync", lambda e, s=s: e.dma_start(
                x_res[:].rearrange("p (t d) -> p t d", t=NT),
                xin[s].rearrange("(t p) d -> p t d", p=128)),
                inc=qSY, inc_n=16)
            barrier()
            if stop_after == "setup":
                pass
            el_dummy = None
            for sub in range(n_sub if stop_after not in ("setup", "c0", "c1", "c1a", "c1b", "c1e", "c1f", "setup_nb1", "setup_nball", "setup_ne", "setup_nt") else 0):
                kind = ("attn_a", "ffn", "attn_b", "ffn")[sub]
                try:
                    if kind == "attn_a":
                        emit_attn_a(s)
                    elif kind == "attn_b":
                        emit_attn_b(s)
                    else:
                        emit_ffn(s)
                except _StopEmission:
                    pass
                barrier()

            TOP("sync", lambda e, s=s: e.dma_start(
                y[s].rearrange("(t p) d -> p t d", p=128),
                x_res[:].rearrange("p (t d) -> p t d", t=NT)),
                inc=qSY, inc_n=16)
            barrier()

        P_.wait("sync", [(qSY, qSY.n)])

        with nc.Block() as blk:
            for eng in ENGS:
                getattr(blk, eng)(lambda eobj, eng=eng: [f(eobj) for f in P_.q[eng]])

    return nc


def _shard_inputs(inputs):
    x = np.ascontiguousarray(np.asarray(inputs["L0_attn_a_hidden_states"],
                                        dtype=np.float32))
    wmap = {
        "aaWq": "L0_attn_a_Wq", "aaWk": "L0_attn_a_Wk", "aaWv": "L0_attn_a_Wv",
        "aaWo": "L0_attn_a_Wo", "aaW1": "L0_attn_a_W1", "aab1": "L0_attn_a_b1",
        "aaW2": "L0_attn_a_W2", "aab2": "L0_attn_a_b2",
        "abWq": "L0_attn_b_Wq", "abbq": "L0_attn_b_bq", "abWl": "L0_attn_b_Wl",
        "abbl": "L0_attn_b_bl", "abWk": "L0_attn_b_Wk", "abbk": "L0_attn_b_bk",
        "abWv": "L0_attn_b_Wv", "abbv": "L0_attn_b_bv", "abWo": "L0_attn_b_Wo",
        "abbo": "L0_attn_b_bo", "gw": "L0_ffn_gate_w", "w1": "L0_ffn_w1",
        "w2": "L0_ffn_w2",
    }
    shared = {k: np.ascontiguousarray(np.asarray(inputs[v], dtype=np.float32))
              for k, v in wmap.items()}
    shared["c_ident"] = np.eye(128, dtype=ml_dtypes.bfloat16)
    shared["c_identf"] = np.eye(128, dtype=np.float32)
    kk = np.arange(128)[:, None]
    qq = np.arange(256)[None, :]
    shared["c_m0"] = (kk <= qq).astype(ml_dtypes.bfloat16)
    shared["c_m1"] = ((kk + 128) <= qq).astype(ml_dtypes.bfloat16)
    idxp = np.zeros((128, 32), dtype=ml_dtypes.bfloat16)
    idxp[:, 0::2] = np.arange(128, dtype=np.float32)[:, None]
    idxp[:, 1::2] = np.arange(16, dtype=np.float32)[None, :]
    shared["c_idxp"] = idxp
    in_maps = []
    for c in range(8):
        m = dict(shared)
        m["xin"] = np.ascontiguousarray(x[2 * c:2 * c + 2])
        in_maps.append(m)
    return in_maps


def kernel(**inputs):
    nc = build_nc(4)
    in_maps = _shard_inputs(inputs)
    res = run_bass_kernel_spmd(nc, in_maps, core_ids=list(range(8)))
    return np.concatenate([np.asarray(r["y"]) for r in res.results], axis=0)


# version 02 (final)
# speedup vs baseline: 1.2220x; 1.2220x over previous
import os
import numpy as np
import ml_dtypes
import concourse.bass as bass
import concourse.mybir as mybir
from concourse.bass_utils import run_bass_kernel_spmd
from contextlib import ExitStack

F32 = mybir.dt.float32
BF16 = mybir.dt.bfloat16
U32 = mybir.dt.uint32
AF = mybir.ActivationFunctionType
ALU = mybir.AluOpType
AX = mybir.AxisListType

T = 2048
NT = 16
D = 1024
ND = 8
H = 16
DH = 64
FF = 4096
NE = 8
KCAP = 512
CTX = 1024
NPOOL = 256
EPS = 1e-5
SHIFT = 12.0
BISECT_ITERS = 40

ENGS = ("tensor", "vector", "scalar", "gpsimd", "sync")

W_NAMES = [
    "aaWq", "aaWk", "aaWv", "aaWo", "aaW1", "aab1", "aaW2", "aab2",
    "abWq", "abbq", "abWl", "abbl", "abWk", "abbk", "abWv", "abbv",
    "abWo", "abbo", "gw", "w1", "w2",
]


class Sem:

    def __init__(self, es, nc, name):
        self.h = es.enter_context(nc.semaphore(name))
        self.n = 0
        self.name = name


class Prog:

    def __init__(self):
        self.q = {e: [] for e in ENGS}

    def op(self, eng, f, waits=(), inc=None, inc_n=1):
        wl = tuple((s.h, int(v)) for (s, v) in waits)
        if inc is not None:
            inc.n += inc_n
            ih = inc.h
        else:
            ih = None

        def run(e, f=f, wl=wl, ih=ih, inc_n=inc_n):
            for h, v in wl:
                if v > 0:
                    e.wait_ge(h, v)
            inst = f(e)
            if ih is not None:
                assert inst is not None
                inst.then_inc(ih, inc_n)

        self.q[eng].append(run)

    def wait(self, eng, waits):
        self.op(eng, lambda e: None, waits=waits)


class _StopEmission(Exception):
    pass


KNOBS = set(os.environ.get("KNOBS", "").split(","))


def build_nc(n_sub=4, stop_after=None):
    nc = bass.Bass("TRN2", target_bir_lowering=False,
                   detect_race_conditions=False)

    xin = nc.dram_tensor("xin", [2, T, D], F32, kind="ExternalInput")
    y = nc.dram_tensor("y", [2, T, D], F32, kind="ExternalOutput")
    dbg = nc.dram_tensor("dbg", [128, 16384], F32, kind="ExternalOutput")
    dw = {}
    dw["aaWq"] = nc.dram_tensor("aaWq", [D, D], F32, kind="ExternalInput")
    dw["aaWk"] = nc.dram_tensor("aaWk", [D, D], F32, kind="ExternalInput")
    dw["aaWv"] = nc.dram_tensor("aaWv", [D, D], F32, kind="ExternalInput")
    dw["aaWo"] = nc.dram_tensor("aaWo", [D, D], F32, kind="ExternalInput")
    dw["aaW1"] = nc.dram_tensor("aaW1", [D, 512], F32, kind="ExternalInput")
    dw["aab1"] = nc.dram_tensor("aab1", [512], F32, kind="ExternalInput")
    dw["aaW2"] = nc.dram_tensor("aaW2", [512, D], F32, kind="ExternalInput")
    dw["aab2"] = nc.dram_tensor("aab2", [D], F32, kind="ExternalInput")
    dw["abWq"] = nc.dram_tensor("abWq", [D, D], F32, kind="ExternalInput")
    dw["abbq"] = nc.dram_tensor("abbq", [D], F32, kind="ExternalInput")
    dw["abWl"] = nc.dram_tensor("abWl", [D, 256], F32, kind="ExternalInput")
    dw["abbl"] = nc.dram_tensor("abbl", [256], F32, kind="ExternalInput")
    dw["abWk"] = nc.dram_tensor("abWk", [256, D], F32, kind="ExternalInput")
    dw["abbk"] = nc.dram_tensor("abbk", [D], F32, kind="ExternalInput")
    dw["abWv"] = nc.dram_tensor("abWv", [256, D], F32, kind="ExternalInput")
    dw["abbv"] = nc.dram_tensor("abbv", [D], F32, kind="ExternalInput")
    dw["abWo"] = nc.dram_tensor("abWo", [D, D], F32, kind="ExternalInput")
    dw["abbo"] = nc.dram_tensor("abbo", [D], F32, kind="ExternalInput")
    dw["gw"] = nc.dram_tensor("gw", [NE, D], F32, kind="ExternalInput")
    dw["w1"] = nc.dram_tensor("w1", [NE, D, FF], F32, kind="ExternalInput")
    dw["w2"] = nc.dram_tensor("w2", [NE, FF, D], F32, kind="ExternalInput")
    c_ident = nc.dram_tensor("c_ident", [128, 128], BF16, kind="ExternalInput")
    c_identf = nc.dram_tensor("c_identf", [128, 128], F32, kind="ExternalInput")
    c_m0 = nc.dram_tensor("c_m0", [128, 256], BF16, kind="ExternalInput")
    c_m1 = nc.dram_tensor("c_m1", [128, 256], BF16, kind="ExternalInput")
    c_idxp = nc.dram_tensor("c_idxp", [128, 32], BF16, kind="ExternalInput")
    d_ln = nc.dram_tensor("d_ln", [T, D], BF16, kind="Internal")
    d_probs = nc.dram_tensor("d_probs", [T, NE], F32, kind="Internal")
    d_acc = nc.dram_tensor("d_acc", [T, D], F32, kind="Internal")
    d_w1b = nc.dram_tensor("d_w1b", [NE, 128, 32768], BF16, kind="Internal")
    d_w2b = nc.dram_tensor("d_w2b", [NE, 128, 32768], BF16, kind="Internal")

    with ExitStack() as es:
        def sbt(name, shape, dt):
            return es.enter_context(nc.sbuf_tensor(name, shape, dt))

        def pst(name, shape, dt):
            return es.enter_context(nc.psum_tensor(name, shape, dt))

        x_res = sbt("x_res", [128, NT * D], F32)
        A = sbt("A", [128, 16384], BF16)
        B = sbt("B", [128, 16384], BF16)
        C = sbt("C", [128, 16640], BF16)
        Dn_t = sbt("Dn", [128, 4096], F32)
        Dn = Dn_t[:].bitcast(BF16)
        W = sbt("W", [128, 8192], BF16)
        E = sbt("E", [128, 5120], BF16)
        ident = sbt("ident", [128, 128], BF16)
        m0 = sbt("m0", [128, 256], BF16)
        m1 = sbt("m1", [128, 256], BF16)
        idxp = sbt("idxp", [128, 32], BF16)
        gateT = sbt("gateT", [128, 64], BF16)
        ones1 = sbt("ones1", [1, 128], F32)
        ones1b = sbt("ones1b", [1, 128], BF16)
        bias_a = sbt("bias_a", [128, 12], F32)
        bias_b = sbt("bias_b", [128, 18], F32)
        lncol = sbt("lncol", [128, 160], F32)
        probs = sbt("probs", [128, 128], F32)
        rankt = sbt("rankt", [128, 128], F32)
        gall = sbt("gall", [128, 256], F32)
        idxall = sbt("idxall", [128, 32], U32)
        idxf = sbt("idxf", [128, 8], F32)
        smallf = sbt("smallf", [128, 32], F32)
        bis = sbt("bis", [8, 8], F32)

        P = [pst(f"P{i}", [128, 512], F32) for i in range(8)]

        sT = Sem(es, nc, "sT")
        sV = Sem(es, nc, "sV")
        sS = Sem(es, nc, "sS")
        sG = Sem(es, nc, "sG")
        qSY = Sem(es, nc, "qSY")
        qGW = Sem(es, nc, "qGW")
        qGM = Sem(es, nc, "qGM")
        qGS = Sem(es, nc, "qGS")
        qW1 = [Sem(es, nc, f"qW1{i}") for i in range(2)]
        qW2 = [Sem(es, nc, f"qW2{i}") for i in range(2)]
        qXI = [Sem(es, nc, f"qXI{i}") for i in range(2)]
        qWB = Sem(es, nc, "qWB")
        qB2 = [Sem(es, nc, f"qB2{i}") for i in range(2)]

        P_ = Prog()
        ffn_cache = {"built": False, "total": 0}

        def TOP(eng, f, waits=(), inc=None, inc_n=1):
            P_.op(eng, f, waits=waits, inc=inc, inc_n=inc_n)

        def barrier():
            if stop_after == "setup_nball" or "nob" in KNOBS:
                return
            for eng, s in (("vector", sV), ("scalar", sS), ("gpsimd", sG)):
                TOP(eng, lambda e: e.nop(), inc=s)
            tgt = [(sT, sT.n), (sV, sV.n), (sS, sS.n), (sG, sG.n)]
            if "nobd" not in KNOBS:
                tgt += [(qSY, qSY.n), (qGW, qGW.n), (qGM, qGM.n), (qGS, qGS.n)]
                tgt += [(q, q.n) for q in qW1 + qW2 + qXI + qB2]
                tgt += [(qWB, qWB.n)]
            for eng in ENGS:
                P_.wait(eng, tgt)

        def xt(s_unused, t):
            return x_res[:, D * t:D * (t + 1)]

        def check_stop(name):
            if stop_after is not None and name == stop_after:
                raise _StopEmission()

        def load_w_tiled(dst_ap, src, rows, cols, sem, waits=()):
            nrt = rows // 128
            TOP("gpsimd",
                lambda e, d=dst_ap, s=src, nrt=nrt, cols=cols: e.dma_start(
                    d.rearrange("p (rt c) -> p rt c", rt=nrt),
                    s.rearrange("(rt p) c -> p rt c", p=128)),
                waits=waits, inc=sem, inc_n=16)

        def ln_stats(s, tiles, sq_scr):
            n = len(tiles)
            c = lncol
            for i, t in enumerate(tiles):
                TOP("vector", lambda e, t=t, i=i: e.reduce_sum(
                    c[:, i:i + 1], xt(s, t), axis=AX.X),
                    inc=(sV if i == n - 1 else None))
                TOP("scalar", lambda e, t=t, i=i: e.activation(
                    sq_scr, xt(s, t), AF.Square, accum_out=c[:, 16 + i:17 + i]),
                    inc=(sS if i == n - 1 else None))
            TOP("scalar", lambda e: e.drain())
            TOP("scalar", lambda e: e.nop(), inc=sS)
            P_.wait("vector", [(sS, sS.n)])
            TOP("vector", lambda e: e.drain())
            TOP("vector", lambda e, n=n: e.tensor_scalar(
                c[:, 32:32 + n], c[:, 0:n], 1.0 / D, None, ALU.mult))
            TOP("vector", lambda e, n=n: e.tensor_scalar(
                c[:, 48:48 + n], c[:, 16:16 + n], 1.0 / D, EPS,
                ALU.mult, ALU.add))
            TOP("vector", lambda e, n=n: e.tensor_tensor(
                c[:, 64:64 + n], c[:, 32:32 + n], c[:, 32:32 + n], ALU.mult))
            TOP("vector", lambda e, n=n: e.tensor_scalar(
                c[:, 80:80 + n], c[:, 32:32 + n], -1.0, None, ALU.mult))
            TOP("vector", lambda e, n=n: e.scalar_tensor_tensor(
                c[:, 96:96 + n], c[:, 64:64 + n], -1.0, c[:, 48:48 + n],
                ALU.mult, ALU.add))
            TOP("vector", lambda e: e.drain())
            TOP("vector", lambda e: e.nop(), inc=sV)
            P_.wait("scalar", [(sV, sV.n)])
            TOP("scalar", lambda e, n=n: e.activation(
                c[:, 112:112 + n], c[:, 96:96 + n], AF.Ln))
            TOP("scalar", lambda e: e.drain())
            TOP("scalar", lambda e, n=n: e.activation(
                c[:, 128:128 + n], c[:, 112:112 + n], AF.Exp, scale=-0.5))
            TOP("scalar", lambda e: e.drain())
            TOP("scalar", lambda e: e.nop(), inc=sS)
            P_.wait("vector", [(sS, sS.n)])
            TOP("vector", lambda e, n=n: e.tensor_tensor(
                c[:, 144:144 + n], c[:, 80:80 + n], c[:, 128:128 + n],
                ALU.mult))
            TOP("vector", lambda e: e.drain())
            TOP("vector", lambda e: e.nop(), inc=sV)

        def ln_apply(s, t, i, out_bf, waits=()):
            TOP("scalar", lambda e, t=t, i=i, out_bf=out_bf: e.activation(
                out_bf, xt(s, t), AF.Identity, bias=lncol[:, 144 + i:145 + i],
                scale=lncol[:, 128 + i:129 + i]),
                waits=list(waits) + [(sV, sV.n)], inc=sS)
            return sS.n

        def ln_group(s, tiles, sq_scr, bufs, dest_fn, bank0=6):
            ln_stats(s, tiles, sq_scr)
            apply_inc = []
            tr_done = []
            n = len(tiles)
            for i in range(n + 1):
                if i < n:
                    w = []
                    if i >= 2:
                        w.append((sT, tr_done[i - 2]))
                    apply_inc.append(ln_apply(s, tiles[i], i, bufs[i % 2],
                                              waits=w))
                if i == n:
                    TOP("scalar", lambda e: e.drain())
                    TOP("scalar", lambda e: e.nop(), inc=sS)
                if i >= 1:
                    j = i - 1
                    wv = apply_inc[j] + 1 if j < n - 1 else sS.n
                    transpose_tile(bufs[j % 2], dest_fn(j), bank0 + (j % 2),
                                   [(sS, wv)])
                    tr_done.append(sT.n)

        def transpose_tile(src_bf, dest_strided_ap, bank, src_ready,
                           dummy_bank=5, evac=True):
            pb = P[bank][:].bitcast(BF16)
            for j in range(8):
                TOP("tensor",
                    lambda e, j=j: e.transpose(pb[:, 128 * j:128 * (j + 1)],
                                               src_bf[:, 128 * j:128 * (j + 1)],
                                               ident[:]),
                    waits=((list(src_ready) + [(sV, sV.n)]) if j == 0 else ()))
            TOP("tensor", lambda e: e.drain())
            TOP("tensor", lambda e: e.nop(), inc=sT)
            if evac:
                TOP("vector",
                    lambda e: e.tensor_copy(dest_strided_ap,
                                            pb[:].rearrange("p (j c) -> p j c", j=8)),
                    waits=[(sT, sT.n)], inc=sV)

        Df = Dn_t[:]
        ET = [Dn[:, 512 * i:512 * (i + 1)] for i in range(4)]
        recb = [Df[0:64, 1024 + 256 * (2 * i):1024 + 256 * (2 * i) + 512]
                for i in range(2)]
        rrAB = Df[0:1, 2048:3072]

        def attn_scores_out(qbase, qlen, kbase_fn, ktiles, vbase_fn,
                            masked, ybase_t, entry_waits, wo_ready):
            for hp in range(8):
                pvb = [P[4], P[5]]
                for half in range(2):
                    h = 2 * hp + half
                    r0, r1 = (0, 64) if half == 0 else (64, 128)
                    tp = (0, 0) if half == 0 else (64, 0)
                    sb = [P[2 * half], P[2 * half + 1]]
                    exp_counts = []
                    pv_counts = []
                    for kt in range(ktiles):
                        bank = sb[kt % 2]
                        w0 = list(entry_waits) if (hp == 0 and half == 0
                                                   and kt == 0) else []
                        if kt >= 2:
                            w0.append((sS, exp_counts[kt - 2]))
                        else:
                            w0.append((sS, sS.n))
                            w0.append((sV, sV.n))
                        TOP("tensor", lambda e, hp=hp, kt=kt, bank=bank,
                            r0=r0, r1=r1, tp=tp, qbase=qbase, qlen=qlen,
                            kbase_fn=kbase_fn:
                            e.matmul(bank[0:128, 0:qlen],
                                     B[r0:r1, kbase_fn(hp, kt):kbase_fn(hp, kt) + 128],
                                     A[r0:r1, 2048 * hp + qbase:2048 * hp + qbase + qlen],
                                     start=True, stop=True,
                                     tile_position=tp),
                            waits=w0, inc=sT)
                        etb = ET[2 * half + kt % 2]
                        ew = [(sT, sT.n)]
                        if kt >= 2:
                            ew.append((sT, pv_counts[kt - 2]))
                        TOP("scalar", lambda e, bank=bank, etb=etb, qlen=qlen:
                            e.activation(etb[:, 0:qlen], bank[0:128, 0:qlen],
                                         AF.Exp, scale=0.125,
                                         bias=smallf[:, 0:1]),
                            waits=ew, inc=sS)
                        exp_counts.append(sS.n)
                        et_sem = (sS, sS.n)
                        if masked and kt >= ktiles - 2:
                            mt = m0 if kt == ktiles - 2 else m1
                            TOP("vector", lambda e, etb=etb, mt=mt, qlen=qlen:
                                e.tensor_tensor(etb[:, 0:qlen],
                                                etb[:, 0:qlen],
                                                mt[:, 0:qlen], ALU.mult),
                                waits=[(sS, sS.n)], inc=sV)
                            et_sem = (sV, sV.n)
                        pvw = [et_sem]
                        if kt == 0:
                            pvw.append((sV, sV.n))
                        TOP("tensor", lambda e, h=h, kt=kt, half=half,
                            etb=etb, qlen=qlen, vbase_fn=vbase_fn,
                            pv=pvb[half], ktiles=ktiles:
                            e.matmul(pv[0:65, 0:qlen],
                                     C[:, vbase_fn(kt) + 65 * h:vbase_fn(kt) + 65 * h + 65],
                                     etb[:, 0:qlen],
                                     start=(kt == 0), stop=(kt == ktiles - 1)),
                            waits=pvw, inc=sT)
                        pv_counts.append(sT.n)
                    rrb = Dn[0:1, 4096 + 512 * half:4096 + 512 * half + qlen]
                    rrf = Df[0:1, 3072 + 512 * half:3072 + 512 * half + qlen]
                    TOP("scalar", lambda e, rrf=rrf, pv=pvb[half], qlen=qlen:
                        e.activation(rrf, pv[64:65, 0:qlen], AF.Ln),
                        waits=[(sT, sT.n)], inc=sS)
                    TOP("scalar", lambda e: e.nop())
                    rr32 = rrAB[0:1, 512 * half:512 * half + qlen]
                    TOP("scalar", lambda e, rr32=rr32, rrf=rrf:
                        e.activation(rr32, rrf, AF.Exp, scale=-1.0),
                        inc=sS)
                    bcb = sb[0]
                    TOP("tensor", lambda e, rr32=rr32, bcb=bcb, qlen=qlen:
                        e.matmul(bcb[0:64, 0:qlen], ones1[0:1, 0:64], rr32,
                                 start=True, stop=True),
                        waits=[(sS, sS.n)], inc=sT)
                    TOP("vector", lambda e, bcb=bcb, half=half, qlen=qlen:
                        e.tensor_copy(recb[half][0:64, 0:qlen],
                                      bcb[0:64, 0:qlen]),
                        waits=[(sT, sT.n)], inc=sV)
                    TOP("vector", lambda e, half=half, hp=hp, qlen=qlen,
                        pv=pvb[half], r0=r0, r1=r1:
                        e.tensor_tensor(
                            E[r0:r1, qlen * hp:qlen * hp + qlen],
                            pv[0:64, 0:qlen], recb[half][0:64, 0:qlen],
                            ALU.mult),
                        waits=[(sS, sS.n), (sT, sT.n)], inc=sV)
            y_ready = (sV, sV.n)
            nqi = qlen // 128
            for qi in range(nqi):
                t = ybase_t + qi
                for dn in range(2):
                    bank = P[6 + (qi * 2 + dn) % 2]
                    w0 = [y_ready, (sV, sV.n), wo_ready]
                    for ft in range(8):
                        TOP("tensor", lambda e, qi=qi, dn=dn, ft=ft,
                            bank=bank, qlen=qlen:
                            e.matmul(bank[:, 0:512],
                                     E[:, qlen * ft + 128 * qi:qlen * ft + 128 * qi + 128],
                                     W[:, 1024 * ft + 512 * dn:1024 * ft + 512 * dn + 512],
                                     start=(ft == 0), stop=(ft == 7)),
                            waits=(w0 if ft == 0 else ()),
                            inc=(sT if ft == 7 else None))
                    TOP("vector", lambda e, t=t, dn=dn, bank=bank:
                        e.scalar_tensor_tensor(
                            x_res[:, 1024 * t + 512 * dn:1024 * t + 512 * dn + 512],
                            bank[:, 0:512], 1.0,
                            x_res[:, 1024 * t + 512 * dn:1024 * t + 512 * dn + 512],
                            ALU.mult, ALU.add),
                        waits=[(sT, sT.n)], inc=sV)


        def emit_attn_a(s):
            Df = Dn_t[:]
            Cf = C[:].bitcast(F32)
            LNfm = Dn
            w_ready0 = [(sT, sT.n)]
            if stop_after != "ln_nw" and "now" not in KNOBS:
                load_w_tiled(W[:, 0:4096], dw["aaW1"], D, 512, qGW, waits=w_ready0)
                load_w_tiled(W[:, 4096:8192], dw["aaW2"], 512, D, qGW)
            w12_ready = (qGW, qGW.n)

            sq_scr = Cf[:, 0:1024]
            lntok = [E[:, 0:1024], E[:, 1024:2048]]
            nln = 8
            for kb in KNOBS:
                if kb.startswith("lnt"):
                    nln = int(kb[3:])
            ln_group(s, list(range(nln)), sq_scr, lntok,
                     lambda j: LNfm[:].rearrange("p (k c) -> p k c", k=8)[
                         :, :, 128 * j:128 * (j + 1)])
            lnfm_i_ready = (sV, sV.n)
            if "dumpLN" in KNOBS:
                for ch in range(4):
                    TOP("vector", lambda e, ch=ch: e.tensor_copy(
                        x_res[:, 2048 * ch:2048 * (ch + 1)],
                        Dn[:, 2048 * ch:2048 * (ch + 1)]),
                        waits=[(sV, sV.n), (sS, sS.n)], inc=sV)
                TOP("sync", lambda e: e.dma_start(
                    dbg[0:128, 0:8192].rearrange("p (a b) -> p a b", a=4),
                    x_res[:, 0:8192].rearrange("p (a b) -> p a b", a=4)),
                    waits=[(sV, sV.n)], inc=qSY, inc_n=16)
                raise _StopEmission()
            check_stop("ln0")

            for fm in range(1 if "h1one" in KNOBS else 4):
                for nh in range(1 if "h1one" in KNOBS else 2):
                    bank = P[fm % 2]
                    w0 = [(sS, sS.n), (sV, sV.n), w12_ready, lnfm_i_ready]
                    for dt in range(8):
                        TOP("tensor", lambda e, fm=fm, nh=nh, dt=dt, bank=bank:
                            e.matmul(bank[:, 0:512],
                                     W[:, 512 * dt + 128 * fm:512 * dt + 128 * fm + 128],
                                     LNfm[:, 1024 * dt + 512 * nh:1024 * dt + 512 * nh + 512],
                                     start=(dt == 0), stop=(dt == 7)),
                            waits=(w0 if dt == 0 else ()),
                            inc=(sT if dt == 7 else None))
                    if stop_after != "comp_nm":
                        if "h1vec" in KNOBS:
                            TOP("vector", lambda e, fm=fm, nh=nh, bank=bank:
                                e.tensor_copy(B[:, 1024 * fm + 512 * nh:1024 * fm + 512 * nh + 512],
                                              bank[:, 0:512]),
                                waits=[(sT, sT.n)], inc=sS)
                        else:
                            TOP("scalar", lambda e, fm=fm, nh=nh, bank=bank:
                                e.activation(B[:, 1024 * fm + 512 * nh:1024 * fm + 512 * nh + 512],
                                             bank[:, 0:512], AF.Relu,
                                             bias=bias_a[:, fm:fm + 1]),
                                waits=[(sT, sT.n)], inc=sS)
            Ef32 = E[:].bitcast(F32)
            if stop_after in ("comp_h1", "comp_nm"):
                raise _StopEmission()
            for dt in range(8):
                for nh in range(2):
                    bank = P[2 + ((2 * dt + nh) % 2)]
                    w0 = [(sS, sS.n), (sV, sV.n)]
                    for ft in range(4):
                        TOP("tensor", lambda e, dt=dt, nh=nh, ft=ft, bank=bank:
                            e.matmul(bank[:, 0:512],
                                     W[:, 4096 + 1024 * ft + 128 * dt:4096 + 1024 * ft + 128 * dt + 128],
                                     B[:, 1024 * ft + 512 * nh:1024 * ft + 512 * nh + 512],
                                     start=(ft == 0), stop=(ft == 3)),
                            waits=(w0 if ft == 0 else ()),
                            inc=(sT if ft == 3 else None))
                    scr = Ef32[:, 2048:2560]
                    TOP("scalar", lambda e, dt=dt, bank=bank, scr=scr:
                        e.activation(scr, bank[:, 0:512], AF.Identity,
                                     bias=bias_a[:, 4 + dt:5 + dt]),
                        waits=[(sT, sT.n), (sV, sV.n)], inc=sS)
                    pg = scr.rearrange("p (g f) -> p g f", f=4)
                    pdst = E[:, 2048 + 256 * dt + 128 * nh:2048 + 256 * dt + 128 * nh + 128]
                    psc = Ef32[:, 0:128]
                    if stop_after == "comp_mm":
                        continue
                    TOP("vector", lambda e, pg=pg, psc=psc: e.tensor_tensor(
                        psc, pg[:, :, 0], pg[:, :, 1], ALU.add),
                        waits=[(sS, sS.n)])
                    TOP("vector", lambda e, pg=pg, psc=psc: e.tensor_tensor(
                        psc, psc, pg[:, :, 2], ALU.add))
                    TOP("vector", lambda e, pg=pg, psc=psc: e.tensor_tensor(
                        psc, psc, pg[:, :, 3], ALU.add))
                    TOP("vector", lambda e, psc=psc, pdst=pdst: e.tensor_scalar(
                        pdst, psc, 0.25, None, ALU.mult), inc=sV)
            pooled_ready = (sV, sV.n)
            comp_mm_done = (sT, sT.n)
            if stop_after == "comp_mm":
                raise _StopEmission()
            check_stop("comp")

            load_w_tiled(W[:, 0:8192], dw["aaWq"], D, D, qGW, waits=[comp_mm_done])
            wq_ready = (qGW, qGW.n)
            if "dumpW" in KNOBS:
                P_.wait("vector", [wq_ready])
                for ch in range(4):
                    TOP("vector", lambda e, ch=ch: e.tensor_copy(
                        x_res[:, 2048 * ch:2048 * (ch + 1)],
                        W[:, 2048 * ch:2048 * (ch + 1)]), inc=sV)
                TOP("sync", lambda e: e.dma_start(
                    dbg[0:128, 0:8192].rearrange("p (a b) -> p a b", a=4),
                    x_res[:, 0:8192].rearrange("p (a b) -> p a b", a=4)),
                    waits=[(sV, sV.n)], inc=qSY, inc_n=16)
                raise _StopEmission()
            for ft in range(8):
                for nq in range(2):
                    bank = P[(2 * ft + nq) % 2]
                    w0 = [wq_ready, (sS, sS.n)]
                    for dt in range(8):
                        TOP("tensor", lambda e, ft=ft, nq=nq, dt=dt, bank=bank:
                            e.matmul(bank[:, 0:512],
                                     W[:, 1024 * dt + 128 * ft:1024 * dt + 128 * ft + 128],
                                     LNfm[:, 1024 * dt + 512 * nq:1024 * dt + 512 * nq + 512],
                                     start=(dt == 0), stop=(dt == 7)),
                            waits=(w0 if dt == 0 else ()),
                            inc=(sT if dt == 7 else None))
                    TOP("scalar", lambda e, ft=ft, nq=nq, bank=bank:
                        e.copy(A[:, 2048 * ft + 512 * nq:2048 * ft + 512 * nq + 512],
                               bank[:, 0:512]),
                        waits=[(sT, sT.n)], inc=sS)
            for tc in range(2):
                chunk = B[:, 4096 + 4096 * tc:8192 + 4096 * tc]
                ln_group(s, [8 + 4 * tc + ti for ti in range(4)], sq_scr,
                         lntok,
                         lambda j, chunk=chunk: chunk.rearrange(
                             "p (k c) -> p k c", k=8)[:, :, 128 * j:128 * (j + 1)])
                chunk_ready = (sV, sV.n)
                for ft in range(8):
                    bank = P[ft % 2]
                    w0 = [chunk_ready, (sS, sS.n)]
                    for dt in range(8):
                        TOP("tensor", lambda e, ft=ft, dt=dt, bank=bank, chunk=chunk:
                            e.matmul(bank[:, 0:512],
                                     W[:, 1024 * dt + 128 * ft:1024 * dt + 128 * ft + 128],
                                     chunk[:, 512 * dt:512 * dt + 512],
                                     start=(dt == 0), stop=(dt == 7)),
                            waits=(w0 if dt == 0 else ()),
                            inc=(sT if dt == 7 else None))
                    TOP("scalar", lambda e, ft=ft, tc=tc, bank=bank:
                        e.copy(A[:, 2048 * ft + 1024 + 512 * tc:2048 * ft + 1024 + 512 * tc + 512],
                               bank[:, 0:512]),
                        waits=[(sT, sT.n)], inc=sS)
            q_done_t = (sT, sT.n)
            if "dumpA" in KNOBS:
                for ch in range(8):
                    TOP("vector", lambda e, ch=ch: e.tensor_copy(
                        x_res[:, 2048 * ch:2048 * (ch + 1)],
                        A[:, 2048 * ch:2048 * (ch + 1)]),
                        waits=[(sS, sS.n)], inc=sV)
                TOP("sync", lambda e: e.dma_start(
                    dbg[:].rearrange("p (a b) -> p a b", a=8),
                    x_res[:].rearrange("p (a b) -> p a b", a=8)),
                    waits=[(sV, sV.n)], inc=qSY, inc_n=16)
                raise _StopEmission()
            check_stop("q")

            load_w_tiled(W[:, 0:8192], dw["aaWk"], D, D, qGW, waits=[q_done_t])
            wk_ready = (qGW, qGW.n)
            for ft in range(8):
                for nk in range(2):
                    bank = P[(2 * ft + nk) % 2]
                    w0 = [wk_ready, (sS, sS.n)]
                    for dt in range(8):
                        TOP("tensor", lambda e, ft=ft, nk=nk, dt=dt, bank=bank:
                            e.matmul(bank[:, 0:512],
                                     W[:, 1024 * dt + 128 * ft:1024 * dt + 128 * ft + 128],
                                     LNfm[:, 1024 * dt + 512 * nk:1024 * dt + 512 * nk + 512],
                                     start=(dt == 0), stop=(dt == 7)),
                            waits=(w0 if dt == 0 else ()),
                            inc=(sT if dt == 7 else None))
                    TOP("scalar", lambda e, ft=ft, nk=nk, bank=bank:
                        e.copy(B[:, 1024 * ft + 512 * nk:1024 * ft + 512 * nk + 512],
                               bank[:, 0:512]),
                        waits=[(sT, sT.n)], inc=sS)
            for ft in range(8):
                bank = P[ft % 2]
                w0 = [wk_ready, pooled_ready, (sS, sS.n)]
                for dt in range(8):
                    TOP("tensor", lambda e, ft=ft, dt=dt, bank=bank:
                        e.matmul(bank[:, 0:256],
                                 W[:, 1024 * dt + 128 * ft:1024 * dt + 128 * ft + 128],
                                 E[:, 2048 + 256 * dt:2048 + 256 * dt + 256],
                                 start=(dt == 0), stop=(dt == 7)),
                        waits=(w0 if dt == 0 else ()),
                        inc=(sT if dt == 7 else None))
                TOP("scalar", lambda e, ft=ft, bank=bank:
                    e.copy(B[:, 8192 + 256 * ft:8192 + 256 * ft + 256],
                           bank[:, 0:256]),
                    waits=[(sT, sT.n)], inc=sS)
            k_done_t = (sT, sT.n)
            check_stop("k")

            load_w_tiled(W[:, 0:8192], dw["aaWv"], D, D, qGW, waits=[k_done_t])
            wv_ready = (qGW, qGW.n)
            TOP("vector", lambda e: e.memset(
                C[:, 0:10400].rearrange("p (h c) -> p h c", c=65)[:, :, 64:65],
                1.0), inc=sV)
            for tt in range(8):
                for nv in range(2):
                    bank = P[(2 * tt + nv) % 2]
                    w0 = [wv_ready, (sV, sV.n)]
                    for dt in range(8):
                        TOP("tensor", lambda e, tt=tt, nv=nv, dt=dt, bank=bank:
                            e.matmul(bank[:, 0:512],
                                     LNfm[:, 1024 * dt + 128 * tt:1024 * dt + 128 * tt + 128],
                                     W[:, 1024 * dt + 512 * nv:1024 * dt + 512 * nv + 512],
                                     start=(dt == 0), stop=(dt == 7)),
                            waits=(w0 if dt == 0 else ()),
                            inc=(sT if dt == 7 else None))
                    dst = C[:, 1040 * tt + 520 * nv:1040 * tt + 520 * nv + 520]
                    TOP("vector", lambda e, dst=dst, bank=bank: e.tensor_copy(
                        dst.rearrange("p (h c) -> p h c", c=65)[:, :, 0:64],
                        bank[:, 0:512].rearrange("p (h c) -> p h c", c=64)),
                        waits=[(sT, sT.n)], inc=sV)
            for ct in range(2):
                for nv in range(2):
                    bank = P[(2 * ct + nv) % 2]
                    w0 = [(sV, sV.n)]
                    for dt in range(8):
                        TOP("tensor", lambda e, ct=ct, nv=nv, dt=dt, bank=bank:
                            e.matmul(bank[:, 0:512],
                                     E[:, 2048 + 256 * dt + 128 * ct:2048 + 256 * dt + 128 * ct + 128],
                                     W[:, 1024 * dt + 512 * nv:1024 * dt + 512 * nv + 512],
                                     start=(dt == 0), stop=(dt == 7)),
                            waits=(w0 if dt == 0 else ()),
                            inc=(sT if dt == 7 else None))
                    dst = C[:, 8320 + 1040 * ct + 520 * nv:8320 + 1040 * ct + 520 * nv + 520]
                    TOP("vector", lambda e, dst=dst, bank=bank: e.tensor_copy(
                        dst.rearrange("p (h c) -> p h c", c=65)[:, :, 0:64],
                        bank[:, 0:512].rearrange("p (h c) -> p h c", c=64)),
                        waits=[(sT, sT.n)], inc=sV)
            v_done_t = (sT, sT.n)
            check_stop("v")

            load_w_tiled(W[:, 0:8192], dw["aaWo"], D, D, qGW, waits=[v_done_t])
            wo_ready = (qGW, qGW.n)
            entry = [(sS, sS.n), (sV, sV.n), (sT, sT.n)]
            for qc in range(4):
                attn_scores_out(
                    qbase=256 * qc, qlen=256,
                    kbase_fn=lambda hp, kt: 1024 * hp + 128 * kt,
                    ktiles=2 * qc + 2,
                    vbase_fn=lambda kt: 1040 * kt,
                    masked=True, ybase_t=2 * qc, entry_waits=entry,
                    wo_ready=wo_ready)
                entry = []
                check_stop(f"causal{qc}")
            for qc2 in range(2):
                attn_scores_out(
                    qbase=1024 + 512 * qc2, qlen=512,
                    kbase_fn=lambda hp, kt: 8192 + 256 * hp + 128 * kt,
                    ktiles=2,
                    vbase_fn=lambda kt: 8320 + 1040 * kt,
                    masked=False, ybase_t=8 + 4 * qc2, entry_waits=[],
                    wo_ready=wo_ready)

        def emit_ffn(s):
            Cf = C[:].bitcast(F32)
            Wf = W[:].bitcast(F32)
            Df = Dn_t[:]
            iota_row = Wf[:, 512:1024]
            probsT = Cf[0:8, 0:2048]
            ones8 = Cf[0:8, 2048:4096]
            scanw = Cf[0:8, 4096:6144]
            identf = Wf[:, 1280:1408]
            lnfmt = [E[:, 2048:3072], E[:, 3072:4096]]
            lntok = [E[:, 0:1024], E[:, 1024:2048]]

            TOP("gpsimd", lambda e: e.iota(iota_row, pattern=[[1, 512]], base=0,
                                           channel_multiplier=0,
                                           allow_small_or_imprecise_dtypes=True),
                waits=[(sT, sT.n)], inc=sG)
            TOP("sync", lambda e: e.dma_start(identf, c_identf[:]),
                inc=qSY, inc_n=16)
            P_.wait("tensor", [(qSY, qSY.n), (sG, sG.n)])
            TOP("sync", lambda e: e.dma_start(
                d_acc.rearrange("(t p) d -> p t d", p=128),
                x_res[:].rearrange("p (t d) -> p t d", t=NT)),
                inc=qSY, inc_n=16)
            acc_init_q = qSY.n

            ln_stats(s, list(range(16)), Df[:, 0:1024])
            apply_inc = []
            tr_done = []
            gate_evac = []
            dma_q = []
            for i in range(17):
                if i < 16:
                    w = []
                    if i >= 2:
                        w.append((sT, tr_done[i - 2]))
                        w.append((qSY, dma_q[i - 2]))
                    apply_inc.append(ln_apply(s, i, i, lntok[i % 2], waits=w))
                if i == 16:
                    TOP("scalar", lambda e: e.drain())
                    TOP("scalar", lambda e: e.nop(), inc=sS)
                if i >= 1:
                    j = i - 1
                    wv = apply_inc[j] + 1 if j < 15 else sS.n
                    buf = lntok[j % 2]
                    TOP("sync", lambda e, j=j, buf=buf: e.dma_start(
                        d_ln[128 * j:128 * (j + 1), :], buf),
                        waits=[(sS, wv)], inc=qSY, inc_n=16)
                    dma_q.append(qSY.n)
                    transpose_tile(buf, lnfmt[j % 2][:].rearrange(
                        "p (k c) -> p k c", k=8), 6 + (j % 2), [(sS, wv)])
                    tr_done.append(sT.n)
                    gw0 = [(sV, sV.n)]
                    if j >= 1:
                        gw0.append((sS, gate_evac[j - 1]))
                    for dt in range(8):
                        TOP("tensor", lambda e, j=j, dt=dt: e.matmul(
                            P[4][:, 0:8],
                            lnfmt[j % 2][:, 128 * dt:128 * (dt + 1)],
                            gateT[:, 8 * dt:8 * (dt + 1)],
                            start=(dt == 0), stop=(dt == 7)),
                            waits=(gw0 if dt == 0 else ()),
                            inc=(sT if dt == 7 else None))
                    TOP("scalar", lambda e, j=j: e.activation(
                        probs[:, 8 * j:8 * (j + 1)], P[4][:, 0:8], AF.Exp,
                        bias=smallf[:, 0:1]),
                        waits=[(sT, sT.n)], inc=sS)
                    gate_evac.append(sS.n)
            P_.wait("vector", [(sS, sS.n)])
            TOP("vector", lambda e: e.drain())
            for t in range(16):
                TOP("vector", lambda e, t=t: e.reduce_sum(
                    smallf[:, 8 + t:9 + t], probs[:, 8 * t:8 * (t + 1)],
                    axis=AX.X))
            TOP("vector", lambda e: e.drain())
            TOP("vector", lambda e: e.reciprocal(smallf[:, 24:25],
                                                 smallf[:, 8:9]))
            for t in range(16):
                TOP("vector", lambda e, t=t: e.reciprocal(
                    smallf[:, 8 + t:9 + t], smallf[:, 8 + t:9 + t]))
            TOP("vector", lambda e: e.drain())
            for t in range(16):
                TOP("vector", lambda e, t=t: e.tensor_scalar(
                    probs[:, 8 * t:8 * (t + 1)], probs[:, 8 * t:8 * (t + 1)],
                    smallf[:, 8 + t:9 + t], None, ALU.mult),
                    inc=(sV if t == 15 else None))
            TOP("sync", lambda e: e.dma_start(
                d_probs.rearrange("(t p) e -> p t e", p=128),
                probs[:].rearrange("p (t e) -> p t e", t=16)),
                waits=[(sV, sV.n)], inc=qSY, inc_n=16)
            probs_q = qSY.n

            for g4 in range(4):
                for tt in range(4):
                    t = 4 * g4 + tt
                    TOP("tensor", lambda e, t=t, tt=tt, g4=g4: e.transpose(
                        P[6 + (g4 % 2)][0:8, 128 * tt:128 * (tt + 1)],
                        probs[:, 8 * t:8 * (t + 1)], identf[0:128, 0:128]),
                        waits=([(sV, sV.n)] if tt == 0 else ()))
                TOP("tensor", lambda e: e.drain())
                TOP("tensor", lambda e: e.nop(), inc=sT)
                TOP("vector", lambda e, g4=g4: e.tensor_copy(
                    probsT[:, 512 * g4:512 * (g4 + 1)],
                    P[6 + (g4 % 2)][0:8, 0:512]),
                    waits=[(sT, sT.n)], inc=sV)

            b = bis
            TOP("vector", lambda e: e.memset(b[0:8, 0:1], 0.0))
            TOP("vector", lambda e: e.memset(b[0:8, 1:2], 1.0))
            TOP("vector", lambda e: e.memset(b[0:8, 2:3], 0.5))
            TOP("vector", lambda e: e.drain())
            TOP("vector", lambda e: e.nop(), inc=sV)
            for it in range(BISECT_ITERS):
                TOP("vector", lambda e: e.tensor_scalar(
                    scanw, probsT, b[0:8, 2:3], None, ALU.is_ge))
                TOP("vector", lambda e: e.reduce_sum(b[0:8, 5:6], scanw,
                                                     axis=AX.X))
                TOP("vector", lambda e: e.tensor_tensor(
                    b[0:8, 3:4], b[0:8, 1:2], b[0:8, 0:1], ALU.subtract))
                TOP("vector", lambda e: e.drain())
                TOP("vector", lambda e: e.tensor_scalar(
                    b[0:8, 6:7], b[0:8, 5:6], 512.5, None, ALU.is_ge))
                TOP("vector", lambda e: e.tensor_scalar(
                    b[0:8, 4:5], b[0:8, 3:4], 0.5, None, ALU.mult))
                TOP("vector", lambda e: e.drain())
                TOP("vector", lambda e: e.tensor_scalar(
                    b[0:8, 7:8], b[0:8, 6:7], -1.0, 1.0, ALU.mult, ALU.add))
                TOP("vector", lambda e: e.tensor_tensor(
                    b[0:8, 3:4], b[0:8, 4:5], b[0:8, 6:7], ALU.mult))
                TOP("vector", lambda e: e.drain())
                TOP("vector", lambda e: e.tensor_tensor(
                    b[0:8, 0:1], b[0:8, 0:1], b[0:8, 3:4], ALU.add))
                TOP("vector", lambda e: e.tensor_tensor(
                    b[0:8, 3:4], b[0:8, 4:5], b[0:8, 7:8], ALU.mult))
                TOP("vector", lambda e: e.drain())
                TOP("vector", lambda e: e.tensor_tensor(
                    b[0:8, 1:2], b[0:8, 1:2], b[0:8, 3:4], ALU.subtract))
                TOP("vector", lambda e: e.drain())
                TOP("vector", lambda e: e.tensor_tensor(
                    b[0:8, 3:4], b[0:8, 1:2], b[0:8, 0:1], ALU.subtract))
                TOP("vector", lambda e: e.drain())
                TOP("vector", lambda e: e.scalar_tensor_tensor(
                    b[0:8, 2:3], b[0:8, 3:4], 0.5, b[0:8, 0:1],
                    ALU.mult, ALU.add))
                TOP("vector", lambda e: e.drain())
            TOP("vector", lambda e: e.nop(), inc=sV)

            TOP("vector", lambda e: e.memset(ones8, 1.0))
            TOP("vector", lambda e: e.tensor_scalar(
                scanw, probsT, b[0:8, 1:2], None, ALU.is_ge))
            TOP("vector", lambda e: e.drain())
            TOP("vector", lambda e: e.tensor_tensor_scan(
                Cf[0:8, 6144:8192], scanw, ones8, 0.0, ALU.add, ALU.mult))
            TOP("vector", lambda e: e.drain())
            TOP("vector", lambda e: e.tensor_tensor(
                Cf[0:8, 6144:8192], Cf[0:8, 6144:8192], scanw, ALU.mult))
            TOP("vector", lambda e: e.tensor_scalar(
                scanw, scanw, -600.0, 599.0, ALU.mult, ALU.add))
            TOP("vector", lambda e: e.drain())
            TOP("vector", lambda e: e.tensor_tensor(
                Cf[0:8, 6144:8192], Cf[0:8, 6144:8192], scanw, ALU.add),
                inc=sV)
            for g4 in range(4):
                for tt in range(4):
                    t = 4 * g4 + tt
                    TOP("tensor", lambda e, t=t, tt=tt, g4=g4: e.transpose(
                        P[6 + (g4 % 2)][:, 8 * tt:8 * tt + 8],
                        Cf[0:8, 6144 + 128 * t:6144 + 128 * (t + 1)],
                        identf[0:8, 0:8]),
                        waits=([(sV, sV.n)] if tt == 0 else ()))
                TOP("tensor", lambda e: e.drain())
                TOP("tensor", lambda e: e.nop(), inc=sT)
                TOP("vector", lambda e, g4=g4: e.tensor_copy(
                    rankt[:, 32 * g4:32 * (g4 + 1)],
                    P[6 + (g4 % 2)][:, 0:32]),
                    waits=[(sT, sT.n)], inc=sV)

            cp_done = []
            for e_ in range(NE):
                for t in range(16):
                    w = [(sV, sV.n)] if t == 0 else []
                    if t >= 2 or e_ > 0:
                        w.append((sT, sT.n))
                    TOP("vector", lambda e, e_=e_, t=t: e.tensor_scalar(
                        W[:, 512 * (t % 2):512 * (t % 2) + 512],
                        iota_row, rankt[:, 8 * t + e_:8 * t + e_ + 1], None,
                        ALU.is_equal),
                        waits=w, inc=sV)
                    for sc in range(4):
                        TOP("tensor", lambda e, t=t, sc=sc: e.matmul(
                            P[sc][:, 0:2],
                            W[:, 512 * (t % 2) + 128 * sc:512 * (t % 2) + 128 * (sc + 1)],
                            idxp[:, 2 * t:2 * (t + 1)],
                            start=(t == 0), stop=(t == 15)),
                            waits=([(sV, sV.n)] if sc == 0 else ()),
                            inc=(sT if sc == 3 else None))
                TOP("tensor", lambda e: e.drain())
                TOP("tensor", lambda e: e.nop(), inc=sT)
                for sc in range(4):
                    TOP("scalar", lambda e, sc=sc: e.copy(
                        idxf[:, 2 * sc:2 * sc + 2], P[sc][:, 0:2]),
                        waits=([(sT, sT.n)] if sc == 0 else ()),
                        inc=(sS if sc == 3 else None))
                TOP("scalar", lambda e: e.drain())
                TOP("scalar", lambda e: e.nop(), inc=sS)
                for sc in range(4):
                    TOP("vector", lambda e, e_=e_, sc=sc: e.scalar_tensor_tensor(
                        smallf[:, 16 + sc:17 + sc], idxf[:, 2 * sc + 1:2 * sc + 2],
                        128.0, idxf[:, 2 * sc:2 * sc + 1], ALU.mult, ALU.add),
                        waits=([(sS, sS.n)] if sc == 0 else ()))
                for sc in range(4):
                    TOP("vector", lambda e, e_=e_, sc=sc: e.tensor_copy(
                        idxall[:, 4 * e_ + sc:4 * e_ + sc + 1],
                        smallf[:, 16 + sc:17 + sc]),
                        inc=(sV if sc == 3 else None))
                cp_done.append(sV.n)
            TOP("vector", lambda e: e.drain())
            TOP("vector", lambda e: e.nop(), inc=sV)
            idx_ready = (sV, sV.n)

            for e_ in range(NE):
                for sc in range(4):
                    TOP("gpsimd", lambda e, e_=e_, sc=sc: e.indirect_dma_start(
                        gall[:, 8 * (4 * e_ + sc):8 * (4 * e_ + sc) + 8], None,
                        d_probs[:], bass.IndirectOffsetOnAxis(
                            idxall[:, 4 * e_ + sc:4 * e_ + sc + 1], 0)),
                        waits=([idx_ready, (qSY, probs_q)]
                               if e_ == 0 and sc == 0 else ()),
                        inc=qGM, inc_n=16)
            g_ready = (qGM, qGM.n)

            first = not ffn_cache["built"]
            h1_evac = {}
            wb1_counts = {}
            wb2_counts = {}
            w2mm_counts = {}
            for e_ in range(NE):
                xw = [(qSY, dma_q[-1])]
                if e_ == 0:
                    xw.append(idx_ready)
                else:
                    xw.append((sT, sT.n))
                for sc in range(4):
                    TOP("gpsimd", lambda e, e_=e_, sc=sc: e.indirect_dma_start(
                        C[:, 1024 * sc:1024 * (sc + 1)], None, d_ln[:],
                        bass.IndirectOffsetOnAxis(
                            idxall[:, 4 * e_ + sc:4 * e_ + sc + 1], 0)),
                        waits=(xw if sc == 0 else ()),
                        inc=qXI[e_ % 2], inc_n=16)
                for sc in range(4):
                    transpose_tile(
                        C[:, 1024 * sc:1024 * (sc + 1)],
                        C[:, 4096:8192].rearrange("p (k c) -> p k c", k=8)[
                            :, :, 128 * sc:128 * (sc + 1)],
                        6 + (sc % 2),
                        [(qXI[e_ % 2], qXI[e_ % 2].n)] if sc == 0 else [])
                xin_ready = (sV, sV.n)
                for fq in range(4):
                    qb = fq % 2
                    w = [(sS, h1_evac.get((e_, fq - 2),
                                          h1_evac.get((e_ - 1, fq + 2), 0)))]
                    if first:
                        wbk = wb1_counts.get((e_, fq - 2),
                                             wb1_counts.get((e_ - 1, fq + 2)))
                        if wbk:
                            w.append((qWB, wbk))
                        TOP("gpsimd", lambda e, e_=e_, fq=fq, qb=qb: e.dma_start(
                            A[:, 8192 * qb:8192 * (qb + 1)].rearrange(
                                "p (rt c) -> p rt c", rt=8),
                            dw["w1"][e_].rearrange("(rt p) c -> p rt c", p=128)[
                                :, :, 1024 * fq:1024 * (fq + 1)]),
                            waits=w, inc=qW1[qb], inc_n=16)
                        w1q_ready = (qW1[qb], qW1[qb].n)
                        wb1_pending = (e_, fq, qb)
                    else:
                        if e_ == 0 and fq == 0:
                            w.append((qWB, ffn_cache["total"]))
                        TOP("gpsimd", lambda e, e_=e_, fq=fq, qb=qb: e.dma_start(
                            A[:, 8192 * qb:8192 * (qb + 1)],
                            d_w1b[e_][:, 8192 * fq:8192 * (fq + 1)]),
                            waits=w, inc=qW1[qb], inc_n=16)
                        w1q_ready = (qW1[qb], qW1[qb].n)
                    for fl in range(8):
                        bank = P[fl % 2]
                        gw = [w1q_ready, xin_ready, (sS, sS.n)]
                        for dt in range(8):
                            TOP("tensor", lambda e, qb=qb, fl=fl, dt=dt,
                                bank=bank: e.matmul(
                                    bank[:, 0:512],
                                    A[:, 8192 * qb + 1024 * dt + 128 * fl:
                                      8192 * qb + 1024 * dt + 128 * (fl + 1)],
                                    C[:, 4096 + 512 * dt:4096 + 512 * (dt + 1)],
                                    start=(dt == 0), stop=(dt == 7)),
                                waits=(gw if dt == 0 else ()),
                                inc=(sT if dt == 7 else None))
                        TOP("scalar", lambda e, fq=fq, fl=fl, bank=bank:
                            e.activation(
                                B[:, 512 * (8 * fq + fl):512 * (8 * fq + fl) + 512],
                                bank[:, 0:512], AF.Gelu),
                            waits=[(sT, sT.n)], inc=sS)
                    h1_evac[(e_, fq)] = sS.n
                    if first:
                        TOP("sync", lambda e, e_=e_, fq=fq, qb=qb: e.dma_start(
                            d_w1b[e_][:, 8192 * fq:8192 * (fq + 1)],
                            A[:, 8192 * qb:8192 * (qb + 1)]),
                            waits=[(sT, sT.n)], inc=qWB, inc_n=16)
                        wb1_counts[(e_, fq)] = qWB.n
                h2w0 = [(sS, sS.n), (sV, sV.n)]
                for ft in range(32):
                    tb = ft % 2
                    if first:
                        w = ([(sT, sT.n)] if ft >= 2 else [])
                        wbk = wb2_counts.get((e_, ft - 2),
                                             wb2_counts.get((e_ - 1, 30 + ft)))
                        if wbk:
                            w.append((qWB, wbk))
                        TOP("gpsimd", lambda e, e_=e_, ft=ft, tb=tb: e.dma_start(
                            C[:, 8192 + 1024 * tb:8192 + 1024 * (tb + 1)],
                            dw["w2"][e_][128 * ft:128 * (ft + 1), :]),
                            waits=w, inc=qW2[tb], inc_n=16)
                        w2t_ready = (qW2[tb], qW2[tb].n)
                    else:
                        w = ([(sT, w2mm_counts[(e_, ft - 2)])] if ft >= 2 else
                             ([(sT, w2mm_counts[(e_ - 1, 30 + ft)])] if e_ > 0
                              else []))
                        if e_ == 0 and ft == 0:
                            w.append((qWB, ffn_cache["total"]))
                        if e_ == 0 and ft < 2:
                            w += [(sV, sV.n), (sS, sS.n)]
                        TOP("sync", lambda e, e_=e_, ft=ft, tb=tb: e.dma_start(
                            C[:, 8192 + 1024 * tb:8192 + 1024 * (tb + 1)],
                            d_w2b[e_][:, 1024 * ft:1024 * (ft + 1)]),
                            waits=w, inc=qB2[tb], inc_n=16)
                        w2t_ready = (qB2[tb], qB2[tb].n)
                    for sm in range(4):
                        for dn in range(2):
                            bank = P[2 * sm + dn]
                            if ft == 0 and sm == 0 and dn == 0:
                                w = list(h2w0) + [w2t_ready]
                            elif sm == 0 and dn == 0:
                                w = [w2t_ready]
                            else:
                                w = []
                            TOP("tensor", lambda e, ft=ft, sm=sm, dn=dn,
                                tb=tb, bank=bank: e.matmul(
                                    bank[:, 0:512],
                                    B[:, 512 * ft + 128 * sm:512 * ft + 128 * (sm + 1)],
                                    C[:, 8192 + 1024 * tb + 512 * dn:
                                      8192 + 1024 * tb + 512 * (dn + 1)],
                                    start=(ft == 0), stop=(ft == 31)),
                                waits=w,
                                inc=(sT if (sm == 3 and dn == 1) else None))
                    w2mm_counts[(e_, ft)] = sT.n
                    if first:
                        TOP("sync", lambda e, e_=e_, ft=ft, tb=tb: e.dma_start(
                            d_w2b[e_][:, 1024 * ft:1024 * (ft + 1)],
                            C[:, 8192 + 1024 * tb:8192 + 1024 * (tb + 1)]),
                            waits=[(sT, w2mm_counts[(e_, ft)])],
                            inc=qWB, inc_n=16)
                        wb2_counts[(e_, ft)] = qWB.n
                ev = []
                for sm in range(4):
                    for dn in range(2):
                        bank = P[2 * sm + dn]
                        w = [(sT, sT.n)]
                        if e_ > 0:
                            w.append((qGS, 64 * e_))
                        TOP("vector", lambda e, e_=e_, sm=sm, dn=dn, bank=bank:
                            e.tensor_scalar(
                                Df[:, 1024 * sm + 512 * dn:1024 * sm + 512 * (dn + 1)],
                                bank[:, 0:512],
                                gall[:, 8 * (4 * e_ + sm) + e_:8 * (4 * e_ + sm) + e_ + 1],
                                None, ALU.mult),
                            waits=(w + [g_ready]) if (sm == 0 and dn == 0) else [],
                            inc=sV)
                ev_done = sV.n
                TOP("vector", lambda e: e.drain())
                TOP("vector", lambda e: e.nop(), inc=sV)
                for sm in range(4):
                    TOP("gpsimd", lambda e, e_=e_, sm=sm: e.indirect_dma_start(
                        d_acc[:], bass.IndirectOffsetOnAxis(
                            idxall[:, 4 * e_ + sm:4 * e_ + sm + 1], 0),
                        Df[:, 1024 * sm:1024 * (sm + 1)], None,
                        compute_op=ALU.add, oob_is_err=True),
                        waits=([(sV, sV.n), (qGS, 64 * e_),
                                (qSY, acc_init_q)] if sm == 0 else ()),
                        inc=qGS, inc_n=16)
            TOP("sync", lambda e: e.dma_start(
                x_res[:].rearrange("p (t d) -> p t d", t=NT),
                d_acc.rearrange("(t p) d -> p t d", p=128)),
                waits=[(qGS, qGS.n)], inc=qSY, inc_n=16)
            P_.wait("sync", [(qSY, qSY.n)])
            if first:
                ffn_cache["built"] = True
                ffn_cache["total"] = qWB.n

        def emit_attn_b(s):
            sq_scr = C[:].bitcast(F32)[:, 1024:2048]
            lntok = [E[:, 0:1024], E[:, 1024:2048]]
            lntok = [B[:, 0:1024], B[:, 1024:2048]]
            load_w_tiled(W[:, 0:8192], dw["abWq"], D, D, qGW,
                         waits=[(sT, sT.n)])
            load_w_tiled(C[:, 0:2048], dw["abWl"], D, 256, qGW)
            wql_ready = (qGW, qGW.n)
            for tc in range(4):
                chunk = Dn[:, 4096 * (tc % 2):4096 * (tc % 2) + 4096]
                ln_group(s, [4 * tc + ti for ti in range(4)], sq_scr, lntok,
                         lambda j, chunk=chunk: chunk.rearrange(
                             "p (k c) -> p k c", k=8)[:, :, 128 * j:128 * (j + 1)])
                chunk_ready = (sV, sV.n)
                for ft in range(8):
                    bank = P[ft % 2]
                    w0 = [chunk_ready, wql_ready, (sS, sS.n)]
                    for dt in range(8):
                        TOP("tensor", lambda e, ft=ft, dt=dt, bank=bank,
                            chunk=chunk: e.matmul(
                                bank[:, 0:512],
                                W[:, 1024 * dt + 128 * ft:1024 * dt + 128 * (ft + 1)],
                                chunk[:, 512 * dt:512 * (dt + 1)],
                                start=(dt == 0), stop=(dt == 7)),
                            waits=(w0 if dt == 0 else ()),
                            inc=(sT if dt == 7 else None))
                    TOP("scalar", lambda e, ft=ft, tc=tc, bank=bank:
                        e.activation(
                            A[:, 2048 * ft + 512 * tc:2048 * ft + 512 * tc + 512],
                            bank[:, 0:512], AF.Identity,
                            bias=bias_b[:, ft:ft + 1]),
                        waits=[(sT, sT.n)], inc=sS)
                for lt in range(2):
                    bank = P[2 + lt]
                    w0 = [(sS, sS.n)]
                    for dt in range(8):
                        TOP("tensor", lambda e, lt=lt, dt=dt, bank=bank,
                            chunk=chunk: e.matmul(
                                bank[:, 0:512],
                                C[:, 256 * dt + 128 * lt:256 * dt + 128 * (lt + 1)],
                                chunk[:, 512 * dt:512 * (dt + 1)],
                                start=(dt == 0), stop=(dt == 7)),
                            waits=(w0 if dt == 0 else ()),
                            inc=(sT if dt == 7 else None))
                    TOP("scalar", lambda e, lt=lt, tc=tc, bank=bank:
                        e.activation(
                            E[:, 2048 * lt + 512 * tc:2048 * lt + 512 * tc + 512],
                            bank[:, 0:512], AF.Identity,
                            bias=bias_b[:, 8 + lt:9 + lt]),
                        waits=[(sT, sT.n)], inc=sS)
            q_lat_t = (sT, sT.n)

            load_w_tiled(W[:, 0:2048], dw["abWk"], 256, D, qGW,
                         waits=[q_lat_t])
            load_w_tiled(W[:, 2048:4096], dw["abWv"], 256, D, qGW)
            wkv_ready = (qGW, qGW.n)
            TOP("sync", lambda e: e.dma_start(
                Df[0:1, 3072:4096], dw["abbv"].rearrange("(o d) -> o d", o=1)),
                waits=[(sV, sV.n), (sS, sS.n)], inc=qSY, inc_n=16)
            TOP("sync", lambda e: e.dma_start(
                Df[0:1, 2048:3072], dw["abbo"].rearrange("(o d) -> o d", o=1)),
                inc=qSY, inc_n=16)
            brow_q = qSY.n
            for half, (dst0, src0) in enumerate([(0, 3072), (1024, 2048)]):
                for nn in range(2):
                    TOP("tensor", lambda e, src0=src0, nn=nn: e.matmul(
                        P[4][:, 0:512], ones1[0:1, 0:128],
                        Df[0:1, src0 + 512 * nn:src0 + 512 * (nn + 1)],
                        start=True, stop=True),
                        waits=[(qSY, brow_q), (sV, sV.n), (sS, sS.n)],
                        inc=sT)
                    TOP("scalar", lambda e, dst0=dst0, nn=nn: e.copy(
                        Df[:, dst0 + 512 * nn:dst0 + 512 * (nn + 1)],
                        P[4][:, 0:512]),
                        waits=[(sT, sT.n)], inc=sS)
            bvb_ready = (sS, sS.n)
            P_.wait("vector", [bvb_ready])
            TOP("vector", lambda e: e.drain())
            for t in range(NT):
                TOP("vector", lambda e, t=t: e.tensor_tensor(
                    x_res[:, 1024 * t:1024 * (t + 1)],
                    x_res[:, 1024 * t:1024 * (t + 1)],
                    Df[:, 1024:2048], ALU.add),
                    inc=(sV if t == NT - 1 else None))
            TOP("vector", lambda e: e.memset(
                C[:, 0:16640].rearrange("p (h c) -> p h c", c=65)[:, :, 64:65],
                1.0), inc=sV)
            for ft in range(8):
                for nk in range(4):
                    bank = P[(2 * ft + nk) % 2]
                    w0 = [wkv_ready, (sS, sS.n)]
                    for lt in range(2):
                        TOP("tensor", lambda e, ft=ft, nk=nk, lt=lt, bank=bank:
                            e.matmul(
                                bank[:, 0:512],
                                W[:, 1024 * lt + 128 * ft:1024 * lt + 128 * (ft + 1)],
                                E[:, 2048 * lt + 512 * nk:2048 * lt + 512 * (nk + 1)],
                                start=(lt == 0), stop=(lt == 1)),
                            waits=(w0 if lt == 0 else ()),
                            inc=(sT if lt == 1 else None))
                    TOP("scalar", lambda e, ft=ft, nk=nk, bank=bank:
                        e.activation(
                            B[:, 2048 * ft + 512 * nk:2048 * ft + 512 * nk + 512],
                            bank[:, 0:512], AF.Identity,
                            bias=bias_b[:, 10 + ft:11 + ft]),
                        waits=[(sT, sT.n)], inc=sS)
            for tt in range(16):
                for nv in range(2):
                    bank = P[2 + (2 * tt + nv) % 2]
                    w0 = [(sV, sV.n)]
                    for lt in range(2):
                        TOP("tensor", lambda e, tt=tt, nv=nv, lt=lt, bank=bank:
                            e.matmul(
                                bank[:, 0:512],
                                E[:, 2048 * lt + 128 * tt:2048 * lt + 128 * (tt + 1)],
                                W[:, 2048 + 1024 * lt + 512 * nv:2048 + 1024 * lt + 512 * (nv + 1)],
                                start=(lt == 0), stop=(lt == 1)),
                            waits=(w0 if lt == 0 else ()),
                            inc=(sT if lt == 1 else None))
                    dst = C[:, 1040 * tt + 520 * nv:1040 * tt + 520 * nv + 520]
                    TOP("vector", lambda e, dst=dst, bank=bank, nv=nv:
                        e.scalar_tensor_tensor(
                            dst.rearrange("p (h c) -> p h c", c=65)[:, :, 0:64],
                            bank[:, 0:512].rearrange("p (h c) -> p h c", c=64),
                            1.0,
                            Df[:, 512 * nv:512 * (nv + 1)].rearrange(
                                "p (h c) -> p h c", c=64),
                            ALU.mult, ALU.add),
                        waits=[(sT, sT.n)], inc=sV)
            v_done_t = (sT, sT.n)

            load_w_tiled(W[:, 0:8192], dw["abWo"], D, D, qGW,
                         waits=[v_done_t])
            wo_ready = (qGW, qGW.n)
            entry = [(sS, sS.n), (sV, sV.n), (sT, sT.n)]
            for qc in range(4):
                attn_scores_out(
                    qbase=512 * qc, qlen=512,
                    kbase_fn=lambda hp, kt: 2048 * hp + 128 * kt,
                    ktiles=16,
                    vbase_fn=lambda kt: 1040 * kt,
                    masked=False, ybase_t=4 * qc, entry_waits=entry,
                    wo_ready=wo_ready)
                entry = []

        TOP("sync", lambda e: e.dma_start(ident[:], c_ident[:]), inc=qSY, inc_n=16)
        if stop_after != "c1b":
            TOP("sync", lambda e: e.dma_start(m0[:], c_m0[:]), inc=qSY, inc_n=16)
            TOP("sync", lambda e: e.dma_start(m1[:], c_m1[:]), inc=qSY, inc_n=16)
            TOP("sync", lambda e: e.dma_start(idxp[:], c_idxp[:]), inc=qSY, inc_n=16)
        g8 = E[0:8, 0:1024]
        TOP("gpsimd", lambda e: e.dma_start(g8, dw["gw"][:]), inc=qGW, inc_n=16)
        for dst, srcname in ([] if stop_after == "c1b" else [
                             (bias_a[:, 0:4], "aab1"), (bias_a[:, 4:12], "aab2"),
                             (bias_b[:, 0:8], "abbq"), (bias_b[:, 8:10], "abbl"),
                             (bias_b[:, 10:18], "abbk")]):
            TOP("sync", lambda e, dst=dst, srcname=srcname: e.dma_start(
                dst, dw[srcname].rearrange("(a p) -> p a", p=128),
                allow_slow_non_contiguous=True), inc=qSY, inc_n=16)
        TOP("vector", lambda e: e.memset(ones1[:], 1.0), inc=sV)
        TOP("vector", lambda e: e.memset(ones1b[:], 1.0), inc=sV)
        TOP("vector", lambda e: e.memset(smallf[:, 0:1], -SHIFT), inc=sV)
        skip_gatet = stop_after in ("c0", "setup_nt") or "ngt" in KNOBS
        skip_gatet_evac = stop_after in ("c0", "c1a", "setup_ne") or "nge" in KNOBS
        skip_extra_consts = stop_after == "c1b"
        gate_ready_w = [(qGW, qGW.n), (qSY, qSY.n)]
        for j in range(8 if not skip_gatet else 0):
            TOP("tensor",
                lambda e, j=j: e.transpose(
                    P[0][:].bitcast(BF16)[:, 8 * j:8 * (j + 1)],
                    g8[:, 128 * j:128 * (j + 1)], ident[0:8, 0:8]),
                waits=(gate_ready_w if j == 0 else ()))
        if not skip_gatet:
            TOP("tensor", lambda e: e.drain())
            TOP("tensor", lambda e: e.nop(), inc=sT)
        if not skip_gatet and not skip_gatet_evac:
            if stop_after == "c1f":
                TOP("tensor", lambda e: e.matmul(P[1][0:8, 0:8], ident[0:8, 0:8],
                                                 ident[0:8, 0:8], start=True,
                                                 stop=True), inc=sT)
            ev_eng = "scalar" if stop_after == "c1e" else "vector"
            if ev_eng == "scalar":
                TOP("scalar", lambda e: e.copy(gateT[:],
                                               P[0][:].bitcast(BF16)[:, 0:64]),
                    waits=[(sT, sT.n)], inc=sS)
            else:
                TOP("vector", lambda e: e.tensor_copy(gateT[:],
                                                      P[0][:].bitcast(BF16)[:, 0:64]),
                    waits=[(sT, sT.n)], inc=sV)
        if stop_after not in ("c1", "c1a", "c1b", "c1e", "c1f", "setup_nb1",
                              "setup_nball"):
            barrier()

        for s in range(2):
            TOP("sync", lambda e, s=s: e.dma_start(
                x_res[:].rearrange("p (t d) -> p t d", t=NT),
                xin[s].rearrange("(t p) d -> p t d", p=128)),
                inc=qSY, inc_n=16)
            barrier()
            if stop_after == "setup":
                pass
            el_dummy = None
            for sub in range(n_sub if stop_after not in ("setup", "c0", "c1", "c1a", "c1b", "c1e", "c1f", "setup_nb1", "setup_nball", "setup_ne", "setup_nt") else 0):
                kind = ("attn_a", "ffn", "attn_b", "ffn")[sub]
                try:
                    if kind == "attn_a":
                        emit_attn_a(s)
                    elif kind == "attn_b":
                        emit_attn_b(s)
                    else:
                        emit_ffn(s)
                except _StopEmission:
                    pass
                barrier()

            TOP("sync", lambda e, s=s: e.dma_start(
                y[s].rearrange("(t p) d -> p t d", p=128),
                x_res[:].rearrange("p (t d) -> p t d", t=NT)),
                inc=qSY, inc_n=16)
            barrier()

        P_.wait("sync", [(qSY, qSY.n)])

        with nc.Block() as blk:
            for eng in ENGS:
                getattr(blk, eng)(lambda eobj, eng=eng: [f(eobj) for f in P_.q[eng]])

    return nc


def _shard_inputs(inputs):
    x = np.ascontiguousarray(np.asarray(inputs["L0_attn_a_hidden_states"],
                                        dtype=np.float32))
    wmap = {
        "aaWq": "L0_attn_a_Wq", "aaWk": "L0_attn_a_Wk", "aaWv": "L0_attn_a_Wv",
        "aaWo": "L0_attn_a_Wo", "aaW1": "L0_attn_a_W1", "aab1": "L0_attn_a_b1",
        "aaW2": "L0_attn_a_W2", "aab2": "L0_attn_a_b2",
        "abWq": "L0_attn_b_Wq", "abbq": "L0_attn_b_bq", "abWl": "L0_attn_b_Wl",
        "abbl": "L0_attn_b_bl", "abWk": "L0_attn_b_Wk", "abbk": "L0_attn_b_bk",
        "abWv": "L0_attn_b_Wv", "abbv": "L0_attn_b_bv", "abWo": "L0_attn_b_Wo",
        "abbo": "L0_attn_b_bo", "gw": "L0_ffn_gate_w", "w1": "L0_ffn_w1",
        "w2": "L0_ffn_w2",
    }
    shared = {k: np.ascontiguousarray(np.asarray(inputs[v], dtype=np.float32))
              for k, v in wmap.items()}
    shared["c_ident"] = np.eye(128, dtype=ml_dtypes.bfloat16)
    shared["c_identf"] = np.eye(128, dtype=np.float32)
    kk = np.arange(128)[:, None]
    qq = np.arange(256)[None, :]
    shared["c_m0"] = (kk <= qq).astype(ml_dtypes.bfloat16)
    shared["c_m1"] = ((kk + 128) <= qq).astype(ml_dtypes.bfloat16)
    idxp = np.zeros((128, 32), dtype=ml_dtypes.bfloat16)
    idxp[:, 0::2] = np.arange(128, dtype=np.float32)[:, None]
    idxp[:, 1::2] = np.arange(16, dtype=np.float32)[None, :]
    shared["c_idxp"] = idxp
    in_maps = []
    for c in range(8):
        m = dict(shared)
        m["xin"] = np.ascontiguousarray(x[2 * c:2 * c + 2])
        in_maps.append(m)
    return in_maps


def kernel(**inputs):
    nc = build_nc(4)
    in_maps = _shard_inputs(inputs)
    res = run_bass_kernel_spmd(nc, in_maps, core_ids=list(range(8)))
    return np.concatenate([np.asarray(r["y"]) for r in res.results], axis=0)
